# version 17
# speedup vs baseline: 1.1786x; 1.0239x over previous
import numpy as np
import ml_dtypes
import concourse.bass as bass
import concourse.mybir as mybir

F32 = mybir.dt.float32
BF16 = mybir.dt.bfloat16
I32 = mybir.dt.int32
AF = mybir.ActivationFunctionType
ALU = mybir.AluOpType
AX = mybir.AxisListType

D = 2048
S = 2048
BPC = 2
T = BPC * S
NKC = D // 128
SQD = float(np.sqrt(2048.0))

SCRATCH = {
    "qkv_bf":    ([4096, D], BF16, "W0", ["A0"]),
    "wout_bf":   ([D, D], BF16, "W0", ["W1"]),
    "gup_bf":    ([16384, D], BF16, "W0", ["F0a"]),
    "wd_bf":     ([D, 8192], BF16, "W0", ["F0b"]),
    "wtildeT_bf": ([D, D], BF16, "W1", ["A0"]),
    "btilde_bf": ([1, D], BF16, "W1", ["A0"]),
    "prep":      ([128, 304], F32, "W1", ["B1"]),
    "ghg_bf":    ([5504, D], BF16, "W3", ["F1a"]),
    "ghu_bf":    ([5504, D], BF16, "W3", ["F1a"]),
    "ghd_bf":    ([D, 5504], BF16, "W3", ["F1b"]),
    "q0s":       ([BPC, D, S], BF16, "A0", ["A1"]),
    "k0s":       ([BPC, D, S], BF16, "A0", ["A1"]),
    "vt0s":      ([BPC, S, D], BF16, "A0", ["A1"]),
    "probs0":    ([BPC, S, S], BF16, "A1", ["A1"]),
    "x1":        ([T, D], F32, "A1", ["F0a", "F0b"]),
    "h0":        ([8192, T], BF16, "F0a", ["F0b"]),
    "x2":        ([T, D], F32, "F0b", ["B0", "B1"]),
    "q1s":       ([BPC, S, 1024], BF16, "B0", ["B1"]),
    "k1s":       ([BPC, S, 1024], BF16, "B0", ["B1"]),
    "v1s":       ([BPC, S, D], BF16, "B0", ["B1"]),
    "gt1":       ([BPC, S, D], BF16, "B0", ["B1"]),
    "om1":       ([BPC, S, D], BF16, "B1", ["B1"]),
    "x3":        ([T, D], F32, "B1", ["F1a", "F1b"]),
    "h1":        ([5504, T], BF16, "F1a", ["F1b"]),
}

MODULE_INPUTS = {
    "x": ([BPC, S, D], F32),
    "w_qkv": ([6144, D], F32),
    "b_qkv": ([6144], F32),
    "w_out": ([D, D], F32),
    "b_out": ([D], F32),
    "w_gup": ([16384, D], F32),
    "w_down": ([D, 8192], F32),
    "wqg": ([D, 3072], F32),
    "wkv": ([D, 3072], F32),
    "wo": ([D, D], F32),
    "ls_iaw": ([16], F32),
    "ls_hws": ([16], F32),
    "gidx": ([5504, D], I32),
    "uidx": ([5504, D], I32),
    "didx": ([D, 5504], I32),
}

PHASE_INPUTS = {
    "W0": ["w_qkv", "w_gup", "w_down", "w_out"],
    "W1": ["w_qkv", "b_qkv", "b_out", "ls_iaw", "ls_hws"],
    "W3": ["gidx", "uidx", "didx"],
    "A0": ["x", "b_qkv"],
    "A1": ["x"],
    "F0a": [],
    "F0b": [],
    "B0": ["wqg", "wkv"],
    "B1": ["wo"],
    "F1a": [],
    "F1b": [],
}

PHASE_ORDER = ["W0", "W1", "W3", "A0", "A1", "F0a", "F0b", "B0", "B1", "F1a", "F1b"]

_NAME_CTR = [0]


def U(name):
    _NAME_CTR[0] += 1
    return f"{name}_{_NAME_CTR[0]}"


class Sem:
    def __init__(self, h):
        self.h = h
        self.n = 0


class Ctx:
    def __init__(self, nc, handles, sems, rings, lut):
        self.nc = nc
        self.h = handles
        self.s = sems
        self.r = rings
        self.lut = lut
        self.misc = {}
        self.ghost = None
        self.ident = None
        self.cast_jobs = []

    CAST_V2 = True

    def init_cast_jobs(self):
        if not Ctx.CAST_V2:
            jobs = []
            for r0 in range(0, 4096, 512):
                jobs.append(("w_qkv", "qkv_bf", r0, 512, None))
            for r0 in range(0, 16384, 2048):
                jobs.append(("w_gup", "gup_bf", r0, 2048, "v1"))
            for r0 in range(0, D, 512):
                jobs.append(("w_down", "wd_bf", r0, 512, "v1"))
            self.cast_jobs = jobs
            self.misc["qkv_tok"] = (self.s["wc1"].h, self.s["wc1"].n + 16 * 8)
            self.misc["w0_rest_tok"] = (self.s["wc2"].h, self.s["wc2"].n + 16 * 12)
            self.misc["w0_rest_done"] = True
            return
        self.cast_ring = [self.s[x] for x in ("wc0", "wc1", "wc2")]
        jobs = []
        for r0 in range(0, 4096, 512):
            jobs.append(("w_qkv", "qkv_bf", r0, 512, None))
        wj = []
        for g in range(16):
            wj.append(("w_gup", "gup_bf", g * 512, 512))
            wj.append(("w_gup", "gup_bf", 8192 + g * 512, 512))
        for r0 in range(0, D, 512):
            wj.append(("w_down", "wd_bf", r0, 512))
        for j, (sn, dn, r0, rr) in enumerate(wj):
            jobs.append((sn, dn, r0, rr, j))
        self.cast_tok = {}
        self.cast_jobs = jobs
        self.misc["qkv_tok"] = (self.s["wc1"].h, self.s["wc1"].n + 16 * 8)
        self.misc["cast_v2"] = True

    def cast_all_tokens(self):
        toks = dict(self.cast_tok)
        simn = [s.n for s in self.cast_ring]
        for sn, dn, r0, rr, j in self.cast_jobs:
            if j is None or j == "v1":
                continue
            simn[j % 3] += 16
            toks[j] = (self.cast_ring[j % 3].h, simn[j % 3])
        return toks

    def cast_step(self, g, n=1):
        while n > 0 and self.cast_jobs:
            sn, dn, r0, rr, j = self.cast_jobs.pop(0)
            if j == "v1":
                dmx(g, self.ap(dn)[r0:r0 + rr, :], self.ap(sn)[r0:r0 + rr, :], self.s["wc2"])
            elif j is None:
                dmx(g, self.ap(dn)[r0:r0 + rr, :], self.ap(sn)[r0:r0 + rr, :], self.s["wc1"])
            else:
                if j >= 3:
                    wait(g, self.cast_tok[j - 3])
                sem = self.cast_ring[j % 3]
                self.cast_tok[j] = dmx(
                    g, self.ap(dn)[r0:r0 + rr, :], self.ap(sn)[r0:r0 + rr, :], sem)
            n -= 1

    def cast_flush_qkv(self, g):
        while self.cast_jobs and self.cast_jobs[0][0] == "w_qkv":
            self.cast_step(g, 1)

    def ap(self, name):
        if name == "y_flat":
            return self.h["y"].ap().rearrange("b s d -> (b s) d")
        return self.h[name].ap()




import contextlib as _ctxlib


def _nullctx():
    return _ctxlib.nullcontext()


GHW = 2752


class Ghost:

    def __init__(self, c):
        self.c = c
        lut = [float(v) for v in c.lut]
        self.lut = lut
        jobs = []
        for name_idx, name_dst, rows, width in [
            ("gidx", "ghg_bf", 5504, D), ("uidx", "ghu_bf", 5504, D),
        ]:
            for r0 in range(0, rows, 128):
                jobs.append((name_idx, name_dst, r0, 0, min(128, rows - r0), width))
        for r0 in range(0, D, 128):
            for hw in range(2):
                jobs.append(("didx", "ghd_bf", r0, hw * GHW, 128, GHW))
        self.jobs = jobs
        self.n = len(jobs)
        hl = c.r["hl"]
        hs = c.r["hs"]
        self.load_tok = [(hl[j % 2].h, 16 * (j // 2 + 1)) for j in range(self.n)]
        self.store_tok = [(hs[j % 2].h, 16 * (j // 2 + 1)) for j in range(self.n)]
        self.hx = c.s["hx"]
        self.OPS = 32
        self.bufs = None
        ng = 86
        self.assign = {
            "A0qk0": (0, 14), "A0qk1": (14, 28),
            "F0a-h0": (28, 58), "F0a-h1": (58, ng),
            "F1a-h0": (ng, 104), "F1a-h1": (104, self.n),
        }

    def chain_cnt(self, j, k=None):
        if k is None:
            return self.OPS * (j + 1)
        return self.OPS * j + k

    def scope(self, tag=""):
        import contextlib
        nc = self.c.nc

        @contextlib.contextmanager
        def _s():
            with (
                nc.sbuf_tensor(U("gh_idx"), [128, 2 * GHW], I32) as idx_sb,
                nc.sbuf_tensor(U("gh_idxb"), [128, 2 * GHW], BF16) as idxb,
                nc.sbuf_tensor(U("gh_term"), [128, GHW], BF16) as term,
                nc.sbuf_tensor(U("gh_acc"), [128, 2 * GHW], BF16) as acc,
            ):
                self.bufs = (idx_sb, idxb, term, acc)
                yield
                self.bufs = None
        return _s()

    def pump(self, sy, a, b):
        c = self.c
        hl = c.r["hl"]
        hs = c.r["hs"]
        idx_sb = self.bufs[0]
        acc = self.bufs[3]
        for j in range(a, min(a + 2, b)):
            nm, dst, r0, c0, rr, w = self.jobs[j]
            if j >= 2:
                wait(sy, self.store_tok[j - 2])
            sy.dma_start(idx_sb[0:rr, (j % 2) * GHW:(j % 2) * GHW + w],
                         c.ap(nm)[r0:r0 + rr, c0:c0 + w]).then_inc(hl[j % 2].h, 16)
            hl[j % 2].n += 16
        for j in range(a, b):
            nm, dst, r0, c0, rr, w = self.jobs[j]
            sy.wait_ge(self.hx.h, self.chain_cnt(j))
            sy.dma_start(c.ap(dst)[r0:r0 + rr, c0:c0 + w],
                         acc[0:rr, (j % 2) * GHW:(j % 2) * GHW + w]).then_inc(hs[j % 2].h, 16)
            hs[j % 2].n += 16
            if j + 2 < b:
                jl = j + 2
                nm2, dst2, r02, c02, rr2, w2 = self.jobs[jl]
                if jl >= 2:
                    wait(sy, self.store_tok[jl - 2])
                sy.dma_start(idx_sb[0:rr2, (jl % 2) * GHW:(jl % 2) * GHW + w2],
                             c.ap(nm2)[r02:r02 + rr2, c02:c02 + w2]).then_inc(hl[jl % 2].h, 16)
                hl[jl % 2].n += 16
        for j in range(max(a, b - 2), b):
            wait(sy, self.store_tok[j])

    def make_pump_iter(self, a, b):
        state = {"j": a, "init": False}
        gh = self

        def f(g, last=False):
            c = gh.c
            hl, hs = c.r["hl"], c.r["hs"]
            idx_sb = gh.bufs[0]
            acc = gh.bufs[3]
            if not state["init"]:
                state["init"] = True
                for j in range(a, min(a + 2, b)):
                    nm, dst, r0, c0, rr, w = gh.jobs[j]
                    if j >= 2:
                        wait(g, gh.store_tok[j - 2])
                    g.dma_start(idx_sb[0:rr, (j % 2) * GHW:(j % 2) * GHW + w],
                                c.ap(nm)[r0:r0 + rr, c0:c0 + w]).then_inc(hl[j % 2].h, 16)
                    hl[j % 2].n += 16
            if state["j"] < b and not last:
                j = state["j"]
                nm, dst, r0, c0, rr, w = gh.jobs[j]
                g.wait_ge(gh.hx.h, gh.chain_cnt(j))
                g.dma_start(c.ap(dst)[r0:r0 + rr, c0:c0 + w],
                            acc[0:rr, (j % 2) * GHW:(j % 2) * GHW + w]).then_inc(hs[j % 2].h, 16)
                hs[j % 2].n += 16
                if j + 2 < b:
                    jl = j + 2
                    nm2, dst2, r02, c02, rr2, w2 = gh.jobs[jl]
                    if jl >= 2:
                        wait(g, gh.store_tok[jl - 2])
                    g.dma_start(idx_sb[0:rr2, (jl % 2) * GHW:(jl % 2) * GHW + w2],
                                c.ap(nm2)[r02:r02 + rr2, c02:c02 + w2]).then_inc(hl[jl % 2].h, 16)
                    hl[jl % 2].n += 16
                state["j"] += 1
            if last:
                while state["j"] < b:
                    j = state["j"]
                    nm, dst, r0, c0, rr, w = gh.jobs[j]
                    g.wait_ge(gh.hx.h, gh.chain_cnt(j))
                    g.dma_start(c.ap(dst)[r0:r0 + rr, c0:c0 + w],
                                acc[0:rr, (j % 2) * GHW:(j % 2) * GHW + w]).then_inc(hs[j % 2].h, 16)
                    hs[j % 2].n += 16
                    if j + 2 < b:
                        jl = j + 2
                        nm2, dst2, r02, c02, rr2, w2 = gh.jobs[jl]
                        if jl >= 2:
                            wait(g, gh.store_tok[jl - 2])
                        g.dma_start(idx_sb[0:rr2, (jl % 2) * GHW:(jl % 2) * GHW + w2],
                                    c.ap(nm2)[r02:r02 + rr2, c02:c02 + w2]).then_inc(hl[jl % 2].h, 16)
                        hl[jl % 2].n += 16
                    state["j"] += 1
                for j in range(max(a, b - 2), b):
                    wait(g, gh.store_tok[j])
        return f

    def make_dve_iter(self, a, b):
        state = {"j": a, "k": 0}
        gh = self

        def f(v, nops):
            idx_sb, idxb, term, acc = gh.bufs
            done = 0
            while done < nops and state["j"] < b:
                j, k = state["j"], state["k"]
                nm, dst, r0, c0, rr, w = gh.jobs[j]
                sl = (j % 2) * GHW
                ii = idxb[0:rr, sl:sl + w]
                ac = acc[0:rr, sl:sl + w]
                tm = term[0:rr, 0:w]
                if k == 0:
                    wait(v, gh.load_tok[j])
                    if j >= 2:
                        wait(v, gh.store_tok[j - 2])
                    v.tensor_copy(ii, idx_sb[0:rr, sl:sl + w]).then_inc(gh.hx.h, 1)
                else:
                    v.wait_ge(gh.hx.h, gh.chain_cnt(j, k))
                    if k == 1:
                        v.tensor_scalar(ac, ii, 0.0, gh.lut[0], op0=ALU.is_equal,
                                        op1=ALU.mult).then_inc(gh.hx.h, 1)
                    elif k % 2 == 0:
                        vv = k // 2
                        v.tensor_scalar(tm, ii, float(vv), gh.lut[vv], op0=ALU.is_equal,
                                        op1=ALU.mult).then_inc(gh.hx.h, 1)
                    else:
                        v.tensor_tensor(ac, ac, tm, op=ALU.add).then_inc(gh.hx.h, 1)
                state["k"] += 1
                done += 1
                if state["k"] == gh.OPS:
                    state["j"] += 1
                    state["k"] = 0
            return (b - state["j"]) * gh.OPS - state["k"]
        return f


def dmx(eng, dst, src, sem, transpose=False):
    if transpose:
        eng.dma_start(dst, src, transpose=True).then_inc(sem.h, 16)
    else:
        eng.dma_start(dst, src).then_inc(sem.h, 16)
    sem.n += 16
    return (sem.h, sem.n)


def wait(eng, tok):
    eng.wait_ge(tok[0], tok[1])


def phase_W0(c):
    pass


def emit_w0_jobs(c, g, sem, names):
    tok = None
    for name_src, name_dst, rows in names:
        src, dst = c.ap(name_src), c.ap(name_dst)
        for r0 in range(0, rows, 2048):
            rr = min(2048, rows - r0)
            tok = dmx(g, dst[r0:r0 + rr, :], src[r0:r0 + rr, :], sem)
    return tok


def phase_W1(c):
    nc = c.nc
    pe, ev, sx = c.s["pe"], c.s["ev"], c.s["sx"]
    g0, w0 = c.r["g"][0], c.r["w"][0]
    with (
        nc.sbuf_tensor(U("wv_sb"), [128, NKC * D], BF16) as wv_sb,
        nc.sbuf_tensor(U("woutT_sb"), [128, NKC * D], BF16) as woutT_sb,
        nc.sbuf_tensor(U("bv_sb"), [128, 16], F32) as bv_sb,
        nc.sbuf_tensor(U("bvb_sb"), [128, 16], BF16) as bvb_sb,
        nc.sbuf_tensor(U("bout_sb"), [1, D], F32) as bout_sb,
        nc.sbuf_tensor(U("ev_sb"), [128, 2 * D], BF16) as ev_sb,
        nc.sbuf_tensor(U("bt_f"), [1, D], F32) as bt_f,
        nc.sbuf_tensor(U("bt_b"), [1, D], BF16) as bt_b,
        nc.sbuf_tensor(U("ls_sb"), [1, 32], F32) as ls_sb,
        nc.sbuf_tensor(U("lse_sb"), [1, 48], F32) as lse_sb,
        nc.sbuf_tensor(U("lsb_sb"), [1, 48], BF16) as lsb_sb,
        nc.sbuf_tensor(U("ones_sb"), [1, 128], BF16) as ones_sb,
        nc.sbuf_tensor(U("prep_sb"), [128, 304], F32) as prep_sb,
        nc.psum_tensor(U("ps_w"), [128, 1024], F32) as ps_w,
        nc.psum_tensor(U("ps_b"), [1, D], F32) as ps_b,
        nc.psum_tensor(U("ps_p"), [128, 64], F32) as ps_p,
    ):
        with nc.Block() as blk:
            wout_toks = []

            @blk.gpsimd
            def _(g):
                wout_toks.append(emit_w0_jobs(c, g, c.s["wc0"], [("w_out", "wout_bf", D)]))
                for j in range(NKC):
                    dmx(g, wv_sb[:, j * D:(j + 1) * D],
                        c.ap("w_qkv")[4096 + j * 128:4096 + (j + 1) * 128, :], g0)
                for j in range(16):
                    dmx(g, bv_sb[:, j:j + 1], c.ap("b_qkv")[4096 + j * 128:4096 + (j + 1) * 128], g0)
                dmx(g, bout_sb[0:1, :], c.ap("b_out")[None, :], g0)
                dmx(g, ls_sb[0:1, 0:16], c.ap("ls_iaw")[None, :], g0)
                tok = dmx(g, ls_sb[0:1, 16:32], c.ap("ls_hws")[None, :], g0)
                g.memset(ones_sb[:, :], 1.0)
                wait(g, tok)
                c.cast_flush_qkv(g)
                if not c.cast_jobs and c.ghost is None:
                    c.misc["qkv_tok"] = emit_w0_jobs(c, g, c.s["wc1"], [("w_qkv", "qkv_bf", 4096)])
                    c.misc["w0_rest_done"] = True
                    c.misc["w0_rest_tok"] = emit_w0_jobs(
                        c, g, c.s["wc2"],
                        [("w_gup", "gup_bf", 16384), ("w_down", "wd_bf", D)])

            @blk.sync
            def _(sy):
                wait(sy, wout_toks[0])
                for j in range(NKC):
                    tok = dmx(sy, woutT_sb[:, j * D:(j + 1) * D],
                              c.ap("wout_bf")[0:D, j * 128:(j + 1) * 128], w0, transpose=True)
                wait(sy, tok)

        with nc.Block() as blk:
            ac = c.s["ac"]

            @blk.scalar
            def _(sc):
                sc.activation(lse_sb[0:1, 0:32], ls_sb[0:1, 0:32], AF.Exp).then_inc(ac.h, 1)
                ac.n += 1

            @blk.vector
            def _(v):
                v.wait_ge(ac.h, ac.n)
                v.tensor_scalar(lse_sb[0:1, 32:48], lse_sb[0:1, 0:16], 1.0, 0.0,
                                op0=ALU.add, op1=ALU.add).then_inc(sx.h, 1)
                sx.n += 1
                v.wait_ge(sx.h, sx.n)
                v.reciprocal(lse_sb[0:1, 32:48], lse_sb[0:1, 32:48]).then_inc(sx.h, 1)
                sx.n += 1
                v.wait_ge(sx.h, sx.n)
                v.tensor_copy(lsb_sb[0:1, :], lse_sb[0:1, :]).then_inc(sx.h, 1)
                v.tensor_copy(bvb_sb[:, :], bv_sb[:, :]).then_inc(sx.h, 1)
                sx.n += 2

        with nc.Block() as blk:
            pe_b, ev_b = pe.n, ev.n
            str_ = c.r["s"]
            st_cnt = [str_[0].n, str_[1].n]

            @blk.tensor
            def _(t):
                for kc in range(NKC):
                    for ob in range(4):
                        gi = kc * 4 + ob
                        if gi >= 2:
                            t.wait_ge(ev.h, ev_b + gi - 1)
                        for j in range(NKC):
                            mm = t.matmul(ps_w[:, (gi % 2) * 512:(gi % 2) * 512 + 512],
                                          wv_sb[:, j * D + kc * 128: j * D + (kc + 1) * 128],
                                          woutT_sb[:, j * D + ob * 512: j * D + (ob + 1) * 512],
                                          start=(j == 0), stop=(j == NKC - 1))
                        mm.then_inc(pe.h, 1)
                        pe.n += 1
                for ob in range(4):
                    for j in range(NKC):
                        mm = t.matmul(ps_b[0:1, ob * 512:(ob + 1) * 512],
                                      bvb_sb[:, j:j + 1],
                                      woutT_sb[:, j * D + ob * 512: j * D + (ob + 1) * 512],
                                      start=(j == 0), stop=(j == NKC - 1))
                    mm.then_inc(pe.h, 1)
                    pe.n += 1
                t.matmul(ps_p[:, 0:48], ones_sb[0:1, :], lsb_sb[0:1, 0:48],
                         start=True, stop=True).then_inc(pe.h, 1)
                pe.n += 1

            @blk.vector
            def _(v):
                for kc in range(NKC):
                    slot = kc % 2
                    if kc >= 2:
                        v.wait_ge(str_[slot].h, st_cnt[slot] + (kc // 2) * 16)
                    for ob in range(4):
                        gi = kc * 4 + ob
                        v.wait_ge(pe.h, pe_b + gi + 1)
                        v.tensor_copy(ev_sb[:, slot * D + ob * 512: slot * D + (ob + 1) * 512],
                                      ps_w[:, (gi % 2) * 512:(gi % 2) * 512 + 512]).then_inc(ev.h, 1)
                        ev.n += 1
                v.wait_ge(pe.h, pe_b + 68)
                v.tensor_tensor(bt_f[0:1, :], ps_b[0:1, :], bout_sb[0:1, :], op=ALU.add).then_inc(sx.h, 1)
                sx.n += 1
                v.wait_ge(sx.h, sx.n)
                v.tensor_copy(bt_b[0:1, :], bt_f[0:1, :]).then_inc(sx.h, 1)
                sx.n += 1
                v.wait_ge(pe.h, pe_b + 69)
                v.tensor_copy(prep_sb[:, 0:16], ps_p[:, 32:48]).then_inc(sx.h, 1)
                sx.n += 1
                v.wait_ge(sx.h, sx.n)
                v.tensor_scalar(prep_sb[:, 16:32], prep_sb[:, 0:16], -1.0, 1.0,
                                op0=ALU.mult, op1=ALU.add).then_inc(sx.h, 1)
                v.tensor_copy(prep_sb[:, 32:48], ps_p[:, 16:32]).then_inc(sx.h, 1)
                sx.n += 2
                v.wait_ge(sx.h, sx.n)
                for rep in range(16):
                    v.tensor_copy(prep_sb[:, 48 + rep * 16:48 + (rep + 1) * 16],
                                  prep_sb[:, 32:48]).then_inc(sx.h, 1)
                    sx.n += 1

            @blk.sync
            def _(sy):
                for kc in range(NKC):
                    slot = kc % 2
                    sy.wait_ge(ev.h, ev_b + (kc + 1) * 4)
                    dmx(sy, c.ap("wtildeT_bf")[kc * 128:(kc + 1) * 128, :],
                        ev_sb[:, slot * D:(slot + 1) * D], str_[slot])
                sy.wait_ge(sx.h, sx.n)
                dmx(sy, c.ap("btilde_bf")[0:1, :], bt_b[0:1, :], str_[2])
                tok = dmx(sy, c.ap("prep")[:, :], prep_sb[:, :], str_[2])
                wait(sy, tok)
                for r in str_[:2]:
                    sy.wait_ge(r.h, r.n)



def phase_W3(c):
    if c.ghost is not None:
        return
    nc = c.nc
    ev = c.s["ev"]
    lut = [float(v) for v in c.lut]
    jobs = []
    for name_idx, name_dst, rows, width in [
        ("gidx", "ghg_bf", 5504, D), ("uidx", "ghu_bf", 5504, D), ("didx", "ghd_bf", D, 5504),
    ]:
        for r0 in range(0, rows, 128):
            rr = min(128, rows - r0)
            jobs.append((c.ap(name_idx)[r0:r0 + rr, :], c.ap(name_dst)[r0:r0 + rr, :], rr, width))
    NB = 2
    WMAX = 5504
    glr = c.r["g"]
    str_ = c.r["s"]
    with (
        nc.sbuf_tensor(U("idx_sb"), [128, NB * WMAX], I32) as idx_sb,
        nc.sbuf_tensor(U("idxb_sb"), [128, NB * WMAX], BF16) as idxb_sb,
        nc.sbuf_tensor(U("gterm"), [128, NB * WMAX], BF16) as gterm,
        nc.sbuf_tensor(U("gacc"), [128, NB * WMAX], BF16) as gacc,
    ):
        load_tok = [None] * len(jobs)
        st_tok = []
        cnt = [str_[0].n, str_[1].n]
        for i in range(len(jobs)):
            b = i % NB
            cnt[b] += 16
            st_tok.append((str_[b].h, cnt[b]))
        ev_b = ev.n
        with nc.Block() as blk:
            @blk.gpsimd
            def _(g):
                for i, (srcap, dstap, rr, width) in enumerate(jobs):
                    b = i % NB
                    if i >= NB:
                        g.wait_ge(ev.h, ev_b + i - 1)
                    load_tok[i] = dmx(g, idx_sb[0:rr, b * WMAX:b * WMAX + width], srcap, glr[b])

            @blk.vector
            def _(v):
                for i, (srcap, dstap, rr, width) in enumerate(jobs):
                    b = i % NB
                    wait(v, load_tok[i])
                    if i >= NB:
                        wait(v, st_tok[i - NB])
                    sx = c.s["sx"]
                    ii = idxb_sb[0:rr, b * WMAX:b * WMAX + width]
                    v.tensor_copy(ii, idx_sb[0:rr, b * WMAX:b * WMAX + width]).then_inc(sx.h, 1)
                    sx.n += 1
                    acc = gacc[0:rr, b * WMAX:b * WMAX + width]
                    tm = gterm[0:rr, b * WMAX:b * WMAX + width]
                    v.wait_ge(sx.h, sx.n)
                    v.tensor_scalar(acc, ii, 0.0, lut[0], op0=ALU.is_equal, op1=ALU.mult).then_inc(sx.h, 1)
                    sx.n += 1
                    for vv in range(1, 16):
                        v.wait_ge(sx.h, sx.n)
                        v.tensor_scalar(tm, ii, float(vv), lut[vv], op0=ALU.is_equal, op1=ALU.mult).then_inc(sx.h, 1)
                        sx.n += 1
                        v.wait_ge(sx.h, sx.n)
                        if vv < 15:
                            v.tensor_tensor(acc, acc, tm, op=ALU.add).then_inc(sx.h, 1)
                            sx.n += 1
                        else:
                            v.tensor_tensor(acc, acc, tm, op=ALU.add).then_inc(ev.h, 1)
                            ev.n += 1

            @blk.sync
            def _(sy):
                for i, (srcap, dstap, rr, width) in enumerate(jobs):
                    b = i % NB
                    sy.wait_ge(ev.h, ev_b + i + 1)
                    dmx(sy, dstap, gacc[0:rr, b * WMAX:b * WMAX + width], str_[b])
                sy.wait_ge(str_[0].h, str_[0].n)
                sy.wait_ge(str_[1].h, str_[1].n)


def emit_layernorm_t(c, src_ap_fn, xlnT, col_stride, ntiles):
    nc = c.nc
    sx, ev, pe, ac = c.s["sx"], c.s["ev"], c.s["pe"], c.s["ac"]
    glr = c.r["g"]
    xw = xlnT.shape[1]
    NQ = D // 512
    with (
        nc.sbuf_tensor(U("ln_x"), [128, 4 * D], F32) as ln_x,
        nc.sbuf_tensor(U("ln_st"), [128, 16 * 32], F32) as ln_st,
        nc.sbuf_tensor(U("ln_o"), [128, 2 * D], BF16) as ln_o,
        nc.psum_tensor(U("ln_pt"), [128, 2 * 1024], BF16) as ln_pt,
    ):
        with nc.Block() as blk:
            pe_b = pe.n
            ld_tok = {}
            gcnt = [glr[k].n for k in range(4)]
            for i in range(ntiles):
                gcnt[i % 4] += 16
                ld_tok[i] = (glr[i % 4].h, gcnt[i % 4])
            sxc = sx.n
            sx_aggr, sx_stt, sx_evac = {}, {}, {}
            for i in range(ntiles + 2):
                if i < ntiles:
                    sxc += NQ + 2
                    sx_aggr[i] = sxc
                if 1 <= i <= ntiles:
                    sxc += 2
                    sx_stt[i - 1] = sxc
                if i >= 2:
                    for kq in range(4):
                        sxc += 1
                        sx_evac[(i - 2, kq)] = sxc
            acc = ac.n
            ac_sqrt, ac_nrm = {}, {}
            for i in range(ntiles + 1):
                if i < ntiles:
                    acc += 1
                    ac_sqrt[i] = acc
                if i >= 1:
                    acc += 1
                    ac_nrm[i - 1] = acc
            pe_tr = {(i, kc): pe_b + 16 * i + kc + 1 for i in range(ntiles) for kc in range(NKC)}

            @blk.gpsimd
            def _(g):
                for i in range(ntiles):
                    if i >= 4:
                        g.wait_ge(ac.h, ac_nrm[i - 4])
                    dmx(g, ln_x[:, (i % 4) * D:(i % 4) * D + D], src_ap_fn(i), glr[i % 4])

            @blk.scalar
            def _(sc):
                for i in range(ntiles + 1):
                    if i < ntiles:
                        s32 = (i % 16) * 32
                        sc.wait_ge(sx.h, sx_aggr[i])
                        sc.activation(ln_st[:, s32 + 26:s32 + 27],
                                      ln_st[:, s32 + 25:s32 + 26],
                                      AF.Sqrt).then_inc(ac.h, 1)
                        ac.n += 1
                    if i >= 1:
                        j = i - 1
                        s32 = (j % 16) * 32
                        sc.wait_ge(sx.h, sx_stt[j])
                        if j >= 2:
                            sc.wait_ge(pe.h, pe_tr[(j - 2, NKC - 1)])
                        sc.activation(ln_o[:, (j % 2) * D:(j % 2) * D + D],
                                      ln_x[:, (j % 4) * D:(j % 4) * D + D],
                                      AF.Identity, bias=ln_st[:, s32 + 27:s32 + 28],
                                      scale=ln_st[:, s32 + 26:s32 + 27]).then_inc(ac.h, 1)
                        ac.n += 1

            @blk.vector
            def _(v):
                for i in range(ntiles + 2):
                    if i < ntiles:
                        s32 = (i % 16) * 32
                        wait(v, ld_tok[i])
                        for q in range(NQ):
                            v.bn_stats(ln_st[:, s32 + 6 * q:s32 + 6 * q + 6],
                                       ln_x[:, (i % 4) * D + q * 512:
                                            (i % 4) * D + (q + 1) * 512]).then_inc(sx.h, 1)
                            sx.n += 1
                        v.wait_ge(sx.h, sx.n)
                        v.bn_aggr(ln_st[:, s32 + 24:s32 + 26],
                                  ln_st[:, s32:s32 + 6 * NQ]).then_inc(sx.h, 1)
                        sx.n += 1
                        v.wait_ge(sx.h, sx.n)
                        v.tensor_scalar(ln_st[:, s32 + 25:s32 + 26],
                                        ln_st[:, s32 + 25:s32 + 26], 1.0, 1e-5,
                                        op0=ALU.mult, op1=ALU.add).then_inc(sx.h, 1)
                        sx.n += 1
                    if 1 <= i <= ntiles:
                        j = i - 1
                        s32 = (j % 16) * 32
                        v.wait_ge(ac.h, ac_sqrt[j])
                        v.reciprocal(ln_st[:, s32 + 26:s32 + 27],
                                     ln_st[:, s32 + 26:s32 + 27]).then_inc(sx.h, 1)
                        sx.n += 1
                        v.wait_ge(sx.h, sx.n)
                        v.scalar_tensor_tensor(ln_st[:, s32 + 27:s32 + 28],
                                               ln_st[:, s32 + 24:s32 + 25], -1.0,
                                               ln_st[:, s32 + 26:s32 + 27],
                                               op0=ALU.mult, op1=ALU.mult).then_inc(sx.h, 1)
                        sx.n += 1
                    if i >= 2:
                        k = i - 2
                        for kq in range(4):
                            v.wait_ge(pe.h, pe_tr[(k, kq * 4 + 3)])
                            dst = bass.AP(xlnT, kq * 4 * col_stride + k * 128,
                                          [[xw, 128], [col_stride, 4], [1, 128]])
                            v.tensor_copy(dst,
                                          ln_pt[:, (kq % 2) * 1024:(kq % 2) * 1024 + 512]).then_inc(sx.h, 1)
                            sx.n += 1

            @blk.tensor
            def _(t):
                for i in range(ntiles):
                    t.wait_ge(ac.h, ac_nrm[i])
                    for kc in range(NKC):
                        kq = kc // 4
                        qg = i * 4 + kq
                        if kc % 4 == 0 and qg >= 2:
                            pq = qg - 2
                            t.wait_ge(sx.h, sx_evac[(pq // 4, pq % 4)])
                        t.transpose(ln_pt[:, (kq % 2) * 1024 + (kc % 4) * 128:
                                          (kq % 2) * 1024 + (kc % 4) * 128 + 128],
                                    ln_o[:, (i % 2) * D + kc * 128:
                                         (i % 2) * D + (kc + 1) * 128],
                                    c.ident[:, :]).then_inc(pe.h, 1)
                        pe.n += 1


def phase_A0(c):
    nc = c.nc
    pe, ev, sx = c.s["pe"], c.s["ev"], c.s["sx"]
    wlr = c.r["w"]
    g0 = c.r["g"][0]
    sar = c.r["a"]
    str_ = c.r["s"]
    with nc.sbuf_tensor(U("bias_sb"), [128, 32], F32) as bias_sb:
        with nc.Block() as blk:
            btoks = []

            @blk.gpsimd
            def _(g):
                for jj in range(32):
                    btoks.append(dmx(g, bias_sb[:, jj:jj + 1],
                                     c.ap("b_qkv")[jj * 128:(jj + 1) * 128], g0))
                wait(g, btoks[-1])

            @blk.vector
            def _(v):
                wait(v, btoks[-1])
                v.tensor_scalar(bias_sb[:, 0:16], bias_sb[:, 0:16], 1.0 / SQD, 0.0,
                                op0=ALU.mult, op1=ALU.add).then_inc(sx.h, 1)
                sx.n += 1
        sx_bias = sx.n

        for sq in range(BPC):
            with (
                nc.sbuf_tensor(U("xlnT"), [128, NKC * S], BF16) as xlnT,
                nc.sbuf_tensor(U("wT_grp"), [128, 3 * 16 * 512], BF16) as wT_grp,
                nc.sbuf_tensor(U("qk_ev"), [128, 4 * 512], BF16) as qk_ev,
                nc.psum_tensor(U("ps_qk"), [128, 2 * 512], F32) as ps_qk,
            ):
                emit_layernorm_t(
                    c, lambda i: c.ap("x").rearrange("b s d -> (b s) d")
                    [sq * S + i * 128: sq * S + (i + 1) * 128, :], xlnT, S, 16)

                gh = c.ghost
                ghr = gh.assign.get(f"A0qk{sq}") if gh else None
                with (gh.scope() if ghr else _nullctx()):
                  with nc.Block() as blk:
                    ngrp = 8
                    grp_tok = [None] * ngrp
                    pe_b, ev_b = pe.n, ev.n
                    sa_cnt = [r.n for r in sar]

                    if ghr:
                        ghf = gh.make_dve_iter(*ghr)
                        ghp = gh.make_pump_iter(*ghr)

                        @blk.vector
                        def _(v):
                            ghf(v, 10 ** 9)

                    if ghr:
                        @blk.gpsimd
                        def _(g):
                            ghp(g, last=True)

                    @blk.sync
                    def _(sy):
                        if c.misc.get("qkv_tok") is not None:
                            wait(sy, c.misc["qkv_tok"])
                        for og in range(ngrp):
                            buf = og % 3
                            if og >= 3:
                                sy.wait_ge(pe.h, pe_b + (og - 2) * 16)
                            for kc in range(NKC):
                                grp_tok[og] = dmx(sy, wT_grp[:, buf * 8192 + kc * 512:buf * 8192 + (kc + 1) * 512],
                                                  c.ap("qkv_bf")[og * 512:(og + 1) * 512, kc * 128:(kc + 1) * 128],
                                                  wlr[buf], transpose=True)

                    @blk.tensor
                    def _(t):
                        t.wait_ge(sx.h, sx_bias)
                        for og in range(ngrp):
                            buf = og % 3
                            wait(t, grp_tok[og])
                            for osub in range(4):
                                for tb in range(4):
                                    gi = og * 16 + osub * 4 + tb
                                    if gi >= 2:
                                        t.wait_ge(ev.h, ev_b + gi - 1)
                                    for kc in range(NKC):
                                        mm = t.matmul(ps_qk[:, (gi % 2) * 512:(gi % 2) * 512 + 512],
                                                      wT_grp[:, buf * 8192 + kc * 512 + osub * 128:
                                                             buf * 8192 + kc * 512 + (osub + 1) * 128],
                                                      xlnT[:, kc * S + tb * 512: kc * S + (tb + 1) * 512],
                                                      start=(kc == 0), stop=(kc == NKC - 1))
                                    mm.then_inc(pe.h, 1)
                                    pe.n += 1

                    @blk.scalar
                    def _(sc):
                        for og in range(ngrp):
                            for osub in range(4):
                                oc = og * 4 + osub
                                for tb in range(4):
                                    gi = og * 16 + osub * 4 + tb
                                    slot = gi % 4
                                    sc.wait_ge(pe.h, pe_b + gi + 1)
                                    if gi >= 4:
                                        sc.wait_ge(sar[slot].h, sa_cnt[slot] + (gi // 4) * 16)
                                    scale = (1.0 / SQD) if og < 4 else 1.0
                                    sc.activation(qk_ev[:, slot * 512:slot * 512 + 512],
                                                  ps_qk[:, (gi % 2) * 512:(gi % 2) * 512 + 512],
                                                  AF.Identity, bias=bias_sb[:, oc:oc + 1],
                                                  scale=scale).then_inc(ev.h, 1)
                                    ev.n += 1
                                    sc.wait_ge(ev.h, ev.n)
                                    dst = c.ap("q0s" if og < 4 else "k0s")
                                    r0 = (oc if og < 4 else oc - 16) * 128
                                    dmx(sc, dst[sq, r0:r0 + 128, tb * 512:(tb + 1) * 512],
                                        qk_ev[:, slot * 512:slot * 512 + 512], sar[slot])
                        for r in sar:
                            sc.wait_ge(r.h, r.n)

                with (
                    nc.sbuf_tensor(U("wtT"), [128, NKC * D], BF16) as wtT,
                    nc.sbuf_tensor(U("btil"), [1, D], BF16) as btil,
                    nc.sbuf_tensor(U("ones1"), [1, 128], BF16) as ones1,
                    nc.sbuf_tensor(U("vt_ev"), [128, 4 * 512], BF16) as vt_ev,
                    nc.psum_tensor(U("ps_vt"), [128, 2 * 512], F32) as ps_vt,
                ):
                    with nc.Block() as blk:
                        pe_b, ev_b = pe.n, ev.n
                        st_cnt = [r.n for r in str_]
                        pl = c.s["pl"]

                        @blk.sync
                        def _(sy):
                            for kc in range(NKC):
                                dmx(sy, wtT[:, kc * D:(kc + 1) * D],
                                    c.ap("wtildeT_bf")[kc * 128:(kc + 1) * 128, :], wlr[0])
                            tok = dmx(sy, btil[0:1, :], c.ap("btilde_bf")[0:1, :], wlr[0])
                            wait(sy, tok)

                        @blk.gpsimd
                        def _(g):
                            g.memset(ones1[:, :], 1.0).then_inc(pl.h, 1)
                            pl.n += 1
                            if c.misc.get("cast_v2"):
                                for q_ in range(4):
                                    g.wait_ge(pe.h, pe_b + 12 * (q_ + 1))
                                    c.cast_step(g, 1)

                        pl_need = pl.n
                        wl_need = (wlr[0].h, wlr[0].n)

                        @blk.tensor
                        def _(t):
                            t.wait_ge(pl.h, pl_need)
                            wait(t, wl_need)
                            for tt in range(16):
                                for ob in range(4):
                                    gi = tt * 4 + ob
                                    if gi >= 2:
                                        t.wait_ge(ev.h, ev_b + gi - 1)
                                    for kc in range(NKC):
                                        t.matmul(ps_vt[:, (gi % 2) * 512:(gi % 2) * 512 + 512],
                                                 xlnT[:, kc * S + tt * 128: kc * S + (tt + 1) * 128],
                                                 wtT[:, kc * D + ob * 512: kc * D + (ob + 1) * 512],
                                                 start=(kc == 0), stop=False)
                                    t.matmul(ps_vt[:, (gi % 2) * 512:(gi % 2) * 512 + 512],
                                             ones1[0:1, 0:128],
                                             btil[0:1, ob * 512:(ob + 1) * 512],
                                             start=False, stop=True).then_inc(pe.h, 1)
                                    pe.n += 1

                        @blk.vector
                        def _(v):
                            for tt in range(16):
                                for ob in range(4):
                                    gi = tt * 4 + ob
                                    slot = gi % 4
                                    v.wait_ge(pe.h, pe_b + gi + 1)
                                    if gi >= 4:
                                        v.wait_ge(str_[slot].h, st_cnt[slot] + (gi // 4) * 16)
                                    v.tensor_copy(vt_ev[:, slot * 512:slot * 512 + 512],
                                                  ps_vt[:, (gi % 2) * 512:(gi % 2) * 512 + 512]).then_inc(ev.h, 1)
                                    ev.n += 1

                        @blk.sync
                        def _(sy):
                            ev_b2 = ev.n - 64
                            for tt in range(16):
                                for ob in range(4):
                                    gi = tt * 4 + ob
                                    slot = gi % 4
                                    sy.wait_ge(ev.h, ev_b2 + gi + 1)
                                    dmx(sy, c.ap("vt0s")[sq, tt * 128:(tt + 1) * 128, ob * 512:(ob + 1) * 512],
                                        vt_ev[:, slot * 512:slot * 512 + 512], str_[slot])
                            for r in str_:
                                sy.wait_ge(r.h, r.n)


def phase_A1(c):
    nc = c.nc
    pe, ev, sx, ac = c.s["pe"], c.s["ev"], c.s["sx"], c.s["ac"]
    wlr, glr, str_ = c.r["w"], c.r["g"], c.r["s"]
    for sq in range(BPC):
        with (
            nc.sbuf_tensor(U("k_sb"), [128, NKC * S], BF16) as k_sb,
            nc.sbuf_tensor(U("vt_sb"), [128, NKC * D], BF16) as vt_sb,
            nc.sbuf_tensor(U("qb_sb"), [128, NKC * 512], BF16) as qb_sb,
            nc.sbuf_tensor(U("exps"), [128, S], F32) as exps,
            nc.sbuf_tensor(U("pr_b"), [128, S], BF16) as pr_b,
            nc.sbuf_tensor(U("pT_sb"), [128, NKC * 512], BF16) as pT_sb,
            nc.sbuf_tensor(U("x0_sb"), [128, 2 * D], F32) as x0_sb,
            nc.sbuf_tensor(U("x1_ev"), [128, 4 * 512], F32) as x1_ev,
            nc.sbuf_tensor(U("sm"), [128, 16], F32) as sm,
            nc.psum_tensor(U("ps_s"), [128, S], F32) as ps_s,
            nc.psum_tensor(U("ps_c"), [128, 2 * 512], F32) as ps_c,
        ):
            with nc.Block() as blk:
                @blk.sync
                def _(sy):
                    for kc in range(NKC):
                        dmx(sy, k_sb[:, kc * S:(kc + 1) * S],
                            c.ap("k0s")[sq, kc * 128:(kc + 1) * 128, :], wlr[0])
                    for kc in range(NKC):
                        tok = dmx(sy, vt_sb[:, kc * D:(kc + 1) * D],
                                  c.ap("vt0s")[sq, kc * 128:(kc + 1) * 128, :], wlr[0])
                    wait(sy, tok)

            with nc.Block() as blk:
                SCHED = []
                for qb in range(6):
                    for qt in range(4):
                        if qb < 4:
                            SCHED.append(("S", qb, qt))
                        if qb >= 2:
                            for ob in range(4):
                                SCHED.append(("C", qb - 2, qt, ob))
                pcnt = pe.n
                sc_done, cx_done_g, cx_done_blk = {}, {}, {}
                for e in SCHED:
                    if e[0] == "S":
                        pcnt += 1
                        sc_done[(e[1], e[2])] = pcnt
                    else:
                        pcnt += 1
                        cx_done_g[(e[1], e[2], e[3])] = pcnt
                        if e[2] == 3 and e[3] == 3:
                            cx_done_blk[e[1]] = pcnt
                scnt, ecnt = sx.n, ev.n
                mx_done, sm_done, cx_ev = {}, {}, {}
                for e in SCHED:
                    if e[0] == "S":
                        scnt += 1
                        mx_done[(e[1], e[2])] = scnt
                        scnt += 3
                        sm_done[(e[1], e[2])] = scnt
                    else:
                        ecnt += 1
                        cx_ev[(e[1], e[2], e[3])] = ecnt
                acnt = ac.n
                exp_done = {}
                for e in SCHED:
                    if e[0] == "S":
                        acnt += 1
                        exp_done[(e[1], e[2])] = acnt
                pstok = {}
                gcnt = glr[0].n
                for qb in range(4):
                    for qt in range(4):
                        gcnt += 16
                        pstok[(qb, qt)] = (glr[0].h, gcnt)
                x0tok = {}
                gx = [glr[2].n, glr[3].n]
                for cb in range(4):
                    for qt in range(4):
                        gx[qt % 2] += 16
                        x0tok[(cb, qt)] = (glr[2 + qt % 2].h, gx[qt % 2])
                x1tok = {}
                stc = [r.n for r in str_]
                for e in SCHED:
                    if e[0] == "C":
                        _, cb, qt, ob = e
                        stc[ob] += 16
                        x1tok[(cb, qt, ob)] = (str_[ob].h, stc[ob])
                qld_tok, pld_tok = {}, {}
                wc = [r.n for r in wlr]
                for qb in range(4):
                    wc[qb % 2] += 16 * 16
                    qld_tok[qb] = (wlr[qb % 2].h, wc[qb % 2])
                for cb in range(4):
                    wc[2 + cb % 2] += 16 * 16
                    pld_tok[cb] = (wlr[2 + cb % 2].h, wc[2 + cb % 2])

                @blk.sync
                def _(sy):
                    for qb in range(6):
                        if qb < 4:
                            if qb >= 1:
                                sy.wait_ge(pe.h, sc_done[(qb - 1, 3)])
                            for kc in range(NKC):
                                dmx(sy, qb_sb[:, kc * 512:(kc + 1) * 512],
                                    c.ap("q0s")[sq, kc * 128:(kc + 1) * 128,
                                                qb * 512:(qb + 1) * 512], wlr[qb % 2])
                        if qb >= 2:
                            cb = qb - 2
                            if cb >= 1:
                                sy.wait_ge(pe.h, cx_done_blk[cb - 1])
                            for qt in range(4):
                                wait(sy, pstok[(cb, qt)])
                            for kc in range(NKC):
                                dmx(sy, pT_sb[:, kc * 512:(kc + 1) * 512],
                                    c.ap("probs0")[sq, cb * 512:(cb + 1) * 512,
                                                   kc * 128:(kc + 1) * 128],
                                    wlr[2 + cb % 2], transpose=True)
                            for qt in range(4):
                                for ob in range(4):
                                    sy.wait_ge(ev.h, cx_ev[(cb, qt, ob)])
                                    dmx(sy, c.ap("x1")[(sq * S + cb * 512 + qt * 128):
                                                       (sq * S + cb * 512 + (qt + 1) * 128),
                                                       ob * 512:(ob + 1) * 512],
                                        x1_ev[:, ob * 512:(ob + 1) * 512], str_[ob])
                    for r in str_:
                        sy.wait_ge(r.h, r.n)

                @blk.gpsimd
                def _(g):
                    for qb in range(6):
                        for qt in range(4):
                            if qb < 4:
                                g.wait_ge(sx.h, sm_done[(qb, qt)])
                                dmx(g, c.ap("probs0")[sq, qb * 512 + qt * 128:
                                                      qb * 512 + (qt + 1) * 128, :],
                                    pr_b[:, :], glr[0])
                                if not Ctx.CAST_V2:
                                    c.cast_step(g, 1)
                                elif qb * 4 + qt < (4 if sq == 0 else 8):
                                    c.cast_step(g, 1)

                            if qb >= 2:
                                cb = qb - 2
                                gi = cb * 4 + qt
                                if gi >= 2:
                                    pc, pq = divmod(gi - 2, 4)
                                    g.wait_ge(ev.h, cx_ev[(pc, pq, 3)])
                                dmx(g, x0_sb[:, (qt % 2) * D:(qt % 2) * D + D],
                                    c.ap("x").rearrange("b s d -> (b s) d")
                                    [sq * S + cb * 512 + qt * 128:
                                     sq * S + cb * 512 + (qt + 1) * 128, :],
                                    glr[2 + qt % 2])
                    g.wait_ge(glr[0].h, glr[0].n)

                @blk.tensor
                def _(t):
                    for e in SCHED:
                        if e[0] == "S":
                            _, qb, qt = e
                            wait(t, qld_tok[qb])
                            if (qb, qt) != (0, 0):
                                pq = (qb, qt - 1) if qt > 0 else (qb - 1, 3)
                                t.wait_ge(ac.h, exp_done[pq])
                            for kb in range(4):
                                for kc in range(NKC):
                                    mm = t.matmul(ps_s[:, kb * 512:(kb + 1) * 512],
                                                  qb_sb[:, kc * 512 + qt * 128: kc * 512 + (qt + 1) * 128],
                                                  k_sb[:, kc * S + kb * 512: kc * S + (kb + 1) * 512],
                                                  start=(kc == 0), stop=(kc == NKC - 1))
                            mm.then_inc(pe.h, 1)
                            pe.n += 1
                        else:
                            _, cb, qt, ob = e
                            wait(t, pld_tok[cb])
                            gi = cb * 16 + qt * 4 + ob
                            if gi >= 2:
                                pg = gi - 2
                                t.wait_ge(ev.h, cx_ev[(pg // 16, (pg % 16) // 4, pg % 4)])
                            for kc in range(NKC):
                                mm = t.matmul(ps_c[:, (gi % 2) * 512:(gi % 2) * 512 + 512],
                                              pT_sb[:, kc * 512 + qt * 128: kc * 512 + (qt + 1) * 128],
                                              vt_sb[:, kc * D + ob * 512: kc * D + (ob + 1) * 512],
                                              start=(kc == 0), stop=(kc == NKC - 1))
                            mm.then_inc(pe.h, 1)
                            pe.n += 1

                @blk.scalar
                def _(sc):
                    for e in SCHED:
                        if e[0] != "S":
                            continue
                        _, qb, qt = e
                        sc.wait_ge(sx.h, mx_done[(qb, qt)])
                        if (qb, qt) != (0, 0):
                            pq = (qb, qt - 1) if qt > 0 else (qb - 1, 3)
                            sc.wait_ge(sx.h, sm_done[pq])
                        sc.activation(exps[:, :], ps_s[:, :], AF.Exp,
                                      bias=sm[:, (qt % 4):(qt % 4) + 1]).then_inc(ac.h, 1)
                        ac.n += 1

                @blk.vector
                def _(v):
                    for e in SCHED:
                        if e[0] == "S":
                            _, qb, qt = e
                            v.wait_ge(pe.h, sc_done[(qb, qt)])
                            if qb >= 1:
                                v.wait_ge(sx.h, sm_done[(qb - 1, qt)])
                            v.reduce_max(sm[:, (qt % 4):(qt % 4) + 1], ps_s[:, :],
                                         axis=AX.X, negate=True).then_inc(sx.h, 1)
                            sx.n += 1
                            v.wait_ge(ac.h, exp_done[(qb, qt)])
                            v.reduce_sum(sm[:, 4 + (qt % 4):5 + (qt % 4)], exps[:, :],
                                         axis=AX.X).then_inc(sx.h, 1)
                            sx.n += 1
                            v.wait_ge(sx.h, sx.n)
                            v.reciprocal(sm[:, 8 + (qt % 4):9 + (qt % 4)],
                                         sm[:, 4 + (qt % 4):5 + (qt % 4)]).then_inc(sx.h, 1)
                            sx.n += 1
                            v.wait_ge(sx.h, sx.n)
                            if (qb, qt) != (0, 0):
                                pq = (qb, qt - 1) if qt > 0 else (qb - 1, 3)
                                wait(v, pstok[pq])
                            v.tensor_scalar(pr_b[:, :], exps[:, :],
                                            sm[:, 8 + (qt % 4):9 + (qt % 4)], 0.0,
                                            op0=ALU.mult, op1=ALU.add).then_inc(sx.h, 1)
                            sx.n += 1
                        else:
                            _, cb, qt, ob = e
                            gi = cb * 16 + qt * 4 + ob
                            v.wait_ge(pe.h, cx_done_g[(cb, qt, ob)])
                            wait(v, x0tok[(cb, qt)])
                            prev4 = gi - 4
                            if prev4 >= 0:
                                wait(v, x1tok[(prev4 // 16, (prev4 % 16) // 4, prev4 % 4)])
                            v.tensor_tensor(x1_ev[:, ob * 512:(ob + 1) * 512],
                                            ps_c[:, (gi % 2) * 512:(gi % 2) * 512 + 512],
                                            x0_sb[:, (qt % 2) * D + ob * 512:
                                                  (qt % 2) * D + (ob + 1) * 512],
                                            op=ALU.add).then_inc(ev.h, 1)
                            ev.n += 1


def emit_ffn_a(c, xln_name, gate_scr, gate_off, up_scr, up_off, inter, h_scr, host_tag=None):
    nc = c.nc
    pe, ev, ac = c.s["pe"], c.s["ev"], c.s["ac"]
    wlr, str_ = c.r["w"], c.r["p"]
    NIC = inter // 128
    groups = [list(range(g, min(g + 4, NIC))) for g in range(0, NIC, 4)]
    for sh in range(BPC):
        gh = c.ghost
        ghr = gh.assign.get(f"{host_tag}-h{sh}") if (gh and host_tag) else None
        with nc.sbuf_tensor(U("fxlnT"), [128, NKC * S], BF16) as xlnT:
            emit_layernorm_t(
                c, lambda i: c.ap(xln_name)[sh * S + i * 128: sh * S + (i + 1) * 128, :],
                xlnT, S, 16)
            with (
                nc.sbuf_tensor(U("wg_sb"), [128, 2 * NKC * 512], BF16) as wg_sb,
                nc.sbuf_tensor(U("wu_sb"), [128, 2 * NKC * 512], BF16) as wu_sb,
                nc.sbuf_tensor(U("sil_sb"), [128, 2 * 512], BF16) as sil_sb,
                nc.sbuf_tensor(U("h_ev"), [128, 4 * 512], BF16) as h_ev,
                nc.psum_tensor(U("ps_g"), [128, 4 * 512], F32) as ps_g,
                nc.psum_tensor(U("ps_u"), [128, 4 * 512], F32) as ps_u,
                (gh.scope() if ghr else _nullctx()),
            ):

                with nc.Block() as blk:
                    pe_b, ev_b, ac_b = pe.n, ev.n, ac.n
                    if ghr:
                        ghf = gh.make_dve_iter(*ghr)
                        ghp = gh.make_pump_iter(*ghr)
                        nops_per_gran = None
                    grp_tok = {}
                    wc = [wlr[0].n, wlr[1].n]
                    for gi_g, chunks in enumerate(groups):
                        wc[gi_g % 2] += 16 * NKC * 2
                        grp_tok[gi_g] = (wlr[gi_g % 2].h, wc[gi_g % 2])
                    GR = []
                    for gi_g, chunks in enumerate(groups):
                        for ci, ic in enumerate(chunks):
                            for tb in range(4):
                                GR.append((gi_g, ci, ic, tb))
                    pe_g = {}
                    p = pe_b
                    for gi, _ in enumerate(GR):
                        pe_g[gi] = (p + 1, p + 2)
                        p += 2
                    ac_g = {}
                    a = ac_b
                    for gi, _ in enumerate(GR):
                        a += 1
                        ac_g[gi] = a
                    ev_g = {}
                    e = ev_b
                    for gi, _ in enumerate(GR):
                        e += 1
                        ev_g[gi] = e
                    sttok = {}
                    stc = [r.n for r in str_]
                    for gi, _ in enumerate(GR):
                        stc[gi % 4] += 16
                        sttok[gi] = (str_[gi % 4].h, stc[gi % 4])

                    gup_tok = None
                    if gate_scr == "gup_bf" and c.misc.get("cast_v2"):
                        _ct = c.cast_all_tokens()
                        gup_tok = {g_: [_ct[2 * g_], _ct[2 * g_ + 1]] for g_ in range(16)}

                    @blk.sync
                    def _(sy):
                        if gup_tok is None and c.misc.get("w0_rest_tok") is not None:
                            wait(sy, c.misc["w0_rest_tok"])
                        for gi_g, chunks in enumerate(groups):
                            buf = gi_g % 2
                            if gup_tok is not None:
                                toks = (gup_tok[gi_g] if sh == 0
                                        else (gup_tok[15] if gi_g == 0 else []))
                                for t_ in toks:
                                    wait(sy, t_)
                            if gi_g >= 2:
                                last_gi = sum(len(gg) * 4 for gg in groups[:gi_g - 1]) - 1
                                sy.wait_ge(pe.h, pe_g[last_gi][1])
                            w = len(chunks) * 128
                            for kc in range(NKC):
                                dmx(sy, wg_sb[:, buf * NKC * 512 + kc * 512:
                                              buf * NKC * 512 + kc * 512 + w],
                                    c.ap(gate_scr)[gate_off + chunks[0] * 128:gate_off + chunks[0] * 128 + w,
                                                   kc * 128:(kc + 1) * 128], wlr[buf], transpose=True)
                                dmx(sy, wu_sb[:, buf * NKC * 512 + kc * 512:
                                              buf * NKC * 512 + kc * 512 + w],
                                    c.ap(up_scr)[up_off + chunks[0] * 128:up_off + chunks[0] * 128 + w,
                                                 kc * 128:(kc + 1) * 128], wlr[buf], transpose=True)

                    @blk.tensor
                    def _(t):
                        for gi, (gi_g, ci, ic, tb) in enumerate(GR):
                            buf = gi_g % 2
                            if ci == 0 and tb == 0:
                                wait(t, grp_tok[gi_g])
                            if gi >= 4:
                                t.wait_ge(ac.h, ac_g[gi - 4])
                                t.wait_ge(ev.h, ev_g[gi - 4])
                            for kc in range(NKC):
                                mm = t.matmul(ps_g[:, (gi % 4) * 512:(gi % 4) * 512 + 512],
                                              wg_sb[:, buf * NKC * 512 + kc * 512 + ci * 128:
                                                    buf * NKC * 512 + kc * 512 + (ci + 1) * 128],
                                              xlnT[:, kc * S + tb * 512: kc * S + (tb + 1) * 512],
                                              start=(kc == 0), stop=(kc == NKC - 1))
                            mm.then_inc(pe.h, 1)
                            pe.n += 1
                            for kc in range(NKC):
                                mm = t.matmul(ps_u[:, (gi % 4) * 512:(gi % 4) * 512 + 512],
                                              wu_sb[:, buf * NKC * 512 + kc * 512 + ci * 128:
                                                    buf * NKC * 512 + kc * 512 + (ci + 1) * 128],
                                              xlnT[:, kc * S + tb * 512: kc * S + (tb + 1) * 512],
                                              start=(kc == 0), stop=(kc == NKC - 1))
                            mm.then_inc(pe.h, 1)
                            pe.n += 1

                    def _store_h(sc, gi):
                        _, _, ic, tb = GR[gi]
                        sc.wait_ge(ev.h, ev_g[gi])
                        dmx(sc, c.ap(h_scr)[ic * 128:(ic + 1) * 128,
                                            sh * S + tb * 512: sh * S + (tb + 1) * 512],
                            h_ev[:, (gi % 4) * 512:(gi % 4) * 512 + 512], str_[gi % 4])

                    @blk.scalar
                    def _(sc):
                        for gi, _ in enumerate(GR):
                            sc.wait_ge(pe.h, pe_g[gi][0])
                            if gi >= 2:
                                sc.wait_ge(ev.h, ev_g[gi - 2])
                            sc.activation(sil_sb[:, (gi % 2) * 512:(gi % 2) * 512 + 512],
                                          ps_g[:, (gi % 4) * 512:(gi % 4) * 512 + 512],
                                          AF.Silu).then_inc(ac.h, 1)
                            ac.n += 1
                            if gi >= 1:
                                _store_h(sc, gi - 1)
                        _store_h(sc, len(GR) - 1)
                        for r in str_:
                            sc.wait_ge(r.h, r.n)

                    @blk.vector
                    def _(v):
                        ng = ((ghr[1] - ghr[0]) * gh.OPS // len(GR) + 1) if ghr else 0
                        for gi, _ in enumerate(GR):
                            v.wait_ge(ac.h, ac_g[gi])
                            v.wait_ge(pe.h, pe_g[gi][1])
                            if gi >= 4:
                                wait(v, sttok[gi - 4])
                            v.tensor_tensor(h_ev[:, (gi % 4) * 512:(gi % 4) * 512 + 512],
                                            sil_sb[:, (gi % 2) * 512:(gi % 2) * 512 + 512],
                                            ps_u[:, (gi % 4) * 512:(gi % 4) * 512 + 512],
                                            op=ALU.mult).then_inc(ev.h, 1)
                            ev.n += 1
                            if ghr:
                                ghf(v, ng)
                        if ghr:
                            ghf(v, 10 ** 9)

                    do_cast = gup_tok is not None and bool(c.cast_jobs)
                    if ghr or do_cast:
                        @blk.gpsimd
                        def _(g):
                            if do_cast and sh == 1 and c.cast_jobs:
                                c.cast_step(g, len(c.cast_jobs))
                            if ghr and do_cast and sh == 0:
                                for _p in range(ghr[1] - ghr[0]):
                                    ghp(g)
                                    if c.cast_jobs:
                                        c.cast_step(g, min(2, len(c.cast_jobs)))
                                ghp(g, last=True)
                            elif ghr:
                                ghp(g, last=True)
                            elif do_cast and sh == 0:
                                while c.cast_jobs:
                                    c.cast_step(g, 1)


def emit_ffn_b(c, h_scr, wdsrc, inter, res_name, out_name):
    nc = c.nc
    pe, ev = c.s["pe"], c.s["ev"]
    wlr, glr, str_ = c.r["w"], c.r["g"], c.r["s"]
    NIC = inter // 128
    for oh in range(2):
        with (
            nc.sbuf_tensor(U("wdT_sb"), [128, NIC * 1024], BF16) as wdT_sb,
            nc.sbuf_tensor(U("h_sb"), [128, 4 * 512], BF16) as h_sb,
            nc.sbuf_tensor(U("res_sb"), [128, 4 * 1024], F32) as res_sb,
            nc.sbuf_tensor(U("o_ev"), [128, 8 * 512], F32) as o_ev,
            nc.psum_tensor(U("ps8"), [128, 8 * 512], F32) as ps8,
        ):
            with nc.Block() as blk:
                @blk.sync
                def _(sy):
                    wd_tok = None
                    if wdsrc == "wd_bf" and c.misc.get("cast_v2"):
                        _ct = c.cast_all_tokens()
                        wd_tok = {0: [_ct[32], _ct[33]], 1: [_ct[34], _ct[35]]}
                    if wd_tok is not None:
                        for t_ in wd_tok[oh]:
                            wait(sy, t_)
                    elif c.misc.get("w0_rest_tok") is not None:
                        wait(sy, c.misc["w0_rest_tok"])
                    for ic in range(NIC):
                        tok = dmx(sy, wdT_sb[:, ic * 1024:(ic + 1) * 1024],
                                  c.ap(wdsrc)[oh * 1024:(oh + 1) * 1024, ic * 128:(ic + 1) * 128],
                                  wlr[3], transpose=True)
                    wait(sy, tok)

            with nc.Block() as blk:
                pe_b, ev_b = pe.n, ev.n
                pe_ic = {(tg, ic): pe_b + tg * NIC + ic + 1 for tg in range(8) for ic in range(NIC)}
                ev_g = {(tg, gr): ev_b + tg * 8 + gr + 1 for tg in range(8) for gr in range(8)}
                htok = {}
                wc = [wlr[0].n, wlr[1].n, wlr[2].n, wlr[3].n]
                for tg in range(8):
                    for ic in range(NIC):
                        k = tg * NIC + ic
                        wc[k % 4] += 16
                        htok[(tg, ic)] = (wlr[k % 4].h, wc[k % 4])
                rtok = {}
                gc = [r.n for r in glr]
                for tg in range(8):
                    for tt in range(4):
                        gc[tt] += 16
                        rtok[(tg, tt)] = (glr[tt].h, gc[tt])
                ring8 = list(str_) + list(c.r["a"])
                sttok = {}
                stc8 = [r.n for r in ring8]
                for tg in range(8):
                    for gr in range(8):
                        stc8[gr] += 16
                        sttok[(tg, gr)] = (ring8[gr].h, stc8[gr])

                @blk.sync
                def _(sy):
                    for tg in range(8):
                        for ic in range(NIC):
                            k = tg * NIC + ic
                            if k >= 4:
                                pk = k - 4
                                sy.wait_ge(pe.h, pe_ic[(pk // NIC, pk % NIC)])
                            dmx(sy, h_sb[:, (k % 4) * 512:(k % 4) * 512 + 512],
                                c.ap(h_scr)[ic * 128:(ic + 1) * 128,
                                            tg * 512:(tg + 1) * 512], wlr[k % 4])

                @blk.gpsimd
                def _(g):
                    for tg in range(8):
                        for tt in range(4):
                            if tg >= 1:
                                g.wait_ge(ev.h, ev_g[(tg - 1, tt * 2 + 1)])
                            dmx(g, res_sb[:, tt * 1024:(tt + 1) * 1024],
                                c.ap(res_name)[tg * 512 + tt * 128: tg * 512 + (tt + 1) * 128,
                                               oh * 1024:(oh + 1) * 1024], glr[tt])

                @blk.tensor
                def _(t):
                    for tg in range(8):
                        for ic in range(NIC):
                            k = tg * NIC + ic
                            wait(t, htok[(tg, ic)])
                            for tt in range(4):
                                for og in range(2):
                                    if tg >= 1 and ic == 0:
                                        t.wait_ge(ev.h, ev_g[(tg - 1, tt * 2 + og)])
                                    mm = t.matmul(ps8[:, (tt * 2 + og) * 512:(tt * 2 + og) * 512 + 512],
                                                  h_sb[:, (k % 4) * 512 + tt * 128:
                                                       (k % 4) * 512 + (tt + 1) * 128],
                                                  wdT_sb[:, ic * 1024 + og * 512:
                                                         ic * 1024 + (og + 1) * 512],
                                                  start=(ic == 0), stop=(ic == NIC - 1))
                            mm.then_inc(pe.h, 1)
                            pe.n += 1

                @blk.vector
                def _(v):
                    for tg in range(8):
                        for gr in range(8):
                            tt, og = gr // 2, gr % 2
                            v.wait_ge(pe.h, pe_ic[(tg, NIC - 1)])
                            wait(v, rtok[(tg, tt)])
                            if tg >= 1:
                                wait(v, sttok[(tg - 1, gr)])
                            v.tensor_tensor(o_ev[:, gr * 512:(gr + 1) * 512],
                                            ps8[:, gr * 512:(gr + 1) * 512],
                                            res_sb[:, tt * 1024 + og * 512:
                                                   tt * 1024 + (og + 1) * 512],
                                            op=ALU.add).then_inc(ev.h, 1)
                            ev.n += 1

                @blk.scalar
                def _(sc):
                    for tg in range(8):
                        for gr in range(8):
                            tt, og = gr // 2, gr % 2
                            sc.wait_ge(ev.h, ev_g[(tg, gr)])
                            dmx(sc, c.ap(out_name)[tg * 512 + tt * 128: tg * 512 + (tt + 1) * 128,
                                                   oh * 1024 + og * 512: oh * 1024 + (og + 1) * 512],
                                o_ev[:, gr * 512:(gr + 1) * 512], ring8[gr])
                    for r in ring8:
                        sc.wait_ge(r.h, r.n)


def phase_F0a(c):
    emit_ffn_a(c, "x1", "gup_bf", 0, "gup_bf", 8192, 8192, "h0", host_tag="F0a")


def phase_F0b(c):
    emit_ffn_b(c, "h0", "wd_bf", 8192, "x1", "x2")


def phase_F1a(c):
    emit_ffn_a(c, "x3", "ghg_bf", 0, "ghu_bf", 0, 5504, "h1", host_tag="F1a")


def phase_F1b(c):
    emit_ffn_b(c, "h1", "ghd_bf", 5504, "x3", "y_flat")


B0_PASSES = [0, 1]
B0_DO_FIXUP = True
B0_SIMPLE_EVAC = False
B0_DO_SWEEP_EVAC = True


def phase_B0(c):
    nc = c.nc
    pe, ev, sx, ac = c.s["pe"], c.s["ev"], c.s["sx"], c.s["ac"]
    wlr, glr, str_ = c.r["w"], c.r["g"], c.r["s"]
    EPS = 1e-12

    def fixup(sq, dst, width, ss_off, rs_all, stage, nts):
        with nc.Block() as blk:
            vtok, sbtok = {}, {}
            gc3 = [glr[0].n, glr[1].n]
            stc3 = [str_[0].n, str_[1].n]
            for tt in range(nts):
                gc3[tt % 2] += 16
                vtok[tt] = (glr[tt % 2].h, gc3[tt % 2])
                stc3[tt % 2] += 16
                sbtok[tt] = (str_[tt % 2].h, stc3[tt % 2])
            sx_b3 = sx.n
            nh = width // (width // 16 // (nts and 1) if False else (width // 16))
            HW = width // 16

            @blk.gpsimd
            def _(g):
                for tt in range(nts):
                    if tt >= 2:
                        wait(g, sbtok[tt - 2])
                    dmx(g, stage[:, (tt % 2) * width:(tt % 2) * width + width],
                        c.ap(dst)[sq, tt * 128:(tt + 1) * 128, :], glr[tt % 2])

            @blk.vector
            def _(v):
                for tt in range(nts):
                    wait(v, vtok[tt])
                    for h in range(16):
                        v.tensor_scalar(stage[:, (tt % 2) * width + h * HW:
                                               (tt % 2) * width + (h + 1) * HW],
                                        stage[:, (tt % 2) * width + h * HW:
                                               (tt % 2) * width + (h + 1) * HW],
                                        rs_all[:, ss_off + tt * 16 + h:
                                               ss_off + tt * 16 + h + 1], 0.0,
                                        op0=ALU.mult, op1=ALU.add).then_inc(sx.h, 1)
                        sx.n += 1

            @blk.sync
            def _(sy):
                for tt in range(nts):
                    sy.wait_ge(sx.h, sx_b3 + (tt + 1) * 16)
                    dmx(sy, c.ap(dst)[sq, tt * 128:(tt + 1) * 128, :],
                        stage[:, (tt % 2) * width:(tt % 2) * width + width], str_[tt % 2])
                for r in str_[:2]:
                    sy.wait_ge(r.h, r.n)

    for sq in range(BPC):
        with nc.sbuf_tensor(U("bxlnT"), [128, NKC * S], BF16) as xlnT:
            emit_layernorm_t(
                c, lambda i: c.ap("x2")[sq * S + i * 128: sq * S + (i + 1) * 128, :],
                xlnT, S, 16)

            for pas in B0_PASSES:
                wname = "wqg" if pas == 0 else "wkv"
                with (
                    nc.sbuf_tensor(U("wsb"), [128, NKC * 3072], BF16) as wsb,
                    nc.sbuf_tensor(U("ss_all"), [128, 2 * 256], F32) as ss_all,
                    nc.sbuf_tensor(U("sec_ev"), [128, 2 * 2048], BF16) as sec_ev,
                    nc.sbuf_tensor(U("junk"), [128, 2 * 384], F32) as junk,
                    nc.sbuf_tensor(U("nrm_ev"), [128, 2 * 1024], BF16) as nrm_ev,
                    nc.sbuf_tensor(U("rs_all"), [128, 2 * 256], F32) as rs_all,
                    nc.psum_tensor(U("ps_b0"), [128, 2 * 512], F32) as ps_b0,
                ):
                    with nc.Block() as blk:
                        @blk.gpsimd
                        def _(g):
                            for kc in range(NKC):
                                tok = dmx(g, wsb[:, kc * 3072:(kc + 1) * 3072],
                                          c.ap(wname)[kc * 128:(kc + 1) * 128, :], glr[0])
                            wait(g, tok)

                    with nc.Block() as blk:
                        pe_b, ac_b, sx_b = pe.n, ac.n, sx.n
                        GR = [(tt, og) for tt in range(16) for og in range(8)]
                        pe_g = {gi: pe_b + gi + 1 for gi in range(len(GR))}
                        ACPG = 5 if pas == 0 else 6
                        SXPG = 4 if pas == 0 else 8
                        ac_g = {gi: ac_b + ACPG * (gi + 1) for gi in range(len(GR))}
                        sx_g = {gi: sx_b + SXPG * (gi + 1) for gi in range(len(GR))}
                        sectok, nrmtok = {}, {}
                        stc = [str_[0].n, str_[1].n, str_[2].n, str_[3].n]
                        for tt in range(16):
                            stc[tt % 2] += 16
                            sectok[tt] = (str_[tt % 2].h, stc[tt % 2])
                            stc[2 + tt % 2] += 16
                            nrmtok[tt] = (str_[2 + tt % 2].h, stc[2 + tt % 2])

                        @blk.tensor
                        def _(t):
                            for gi, (tt, og) in enumerate(GR):
                                if gi >= 2:
                                    t.wait_ge(ac.h, ac_g[gi - 2])
                                    t.wait_ge(sx.h, sx_g[gi - 2])
                                for kc in range(NKC):
                                    mm = t.matmul(ps_b0[:, (gi % 2) * 512:(gi % 2) * 512 + 384],
                                                  xlnT[:, kc * S + tt * 128: kc * S + (tt + 1) * 128],
                                                  wsb[:, kc * 3072 + og * 384: kc * 3072 + (og + 1) * 384],
                                                  start=(kc == 0), stop=(kc == NKC - 1))
                                mm.then_inc(pe.h, 1)
                                pe.n += 1

                        @blk.scalar
                        def _(sc):
                            for gi, (tt, og) in enumerate(GR):
                                sc.wait_ge(pe.h, pe_g[gi])
                                if gi >= 2:
                                    sc.wait_ge(sx.h, sx_g[gi - 2])
                                    sc.wait_ge(ac.h, ac_g[gi - 2])
                                if tt >= 2 and og == 0:
                                    wait(sc, sectok[tt - 2])
                                for hh in range(2):
                                    h = og * 2 + hh
                                    base = (gi % 2) * 512 + hh * 192
                                    sc.activation(junk[:, (gi % 2) * 384 + hh * 192:
                                                        (gi % 2) * 384 + hh * 192 + 64],
                                                  ps_b0[:, base:base + 64], AF.Square,
                                                  accum_out=ss_all[:, tt * 16 + h:tt * 16 + h + 1]).then_inc(ac.h, 1)
                                    ac.n += 1
                                    if pas == 0:
                                        sc.activation(sec_ev[:, (tt % 2) * 2048 + h * 128:
                                                             (tt % 2) * 2048 + (h + 1) * 128],
                                                      ps_b0[:, base + 64:base + 192],
                                                      AF.Silu).then_inc(ac.h, 1)
                                        ac.n += 1
                                    else:
                                        sc.activation(junk[:, (gi % 2) * 384 + hh * 192 + 64:
                                                            (gi % 2) * 384 + hh * 192 + 192],
                                                      ps_b0[:, base + 64:base + 192], AF.Square,
                                                      accum_out=ss_all[:, 256 + tt * 16 + h:
                                                                       256 + tt * 16 + h + 1]).then_inc(ac.h, 1)
                                        ac.n += 1
                                c0 = tt * 16 + og * 2
                                sc.activation(rs_all[:, c0:c0 + 2],
                                              ss_all[:, c0:c0 + 2], AF.Sqrt).then_inc(ac.h, 1)
                                ac.n += 1
                                if pas == 1:
                                    sc.activation(rs_all[:, 256 + c0:256 + c0 + 2],
                                                  ss_all[:, 256 + c0:256 + c0 + 2],
                                                  AF.Sqrt).then_inc(ac.h, 1)
                                    ac.n += 1

                        @blk.vector
                        def _(v):
                            for gi, (tt, og) in enumerate(GR):
                                v.wait_ge(ac.h, ac_g[gi])
                                if gi >= 2:
                                    v.wait_ge(sx.h, sx_g[gi - 2])
                                if tt >= 2 and og == 0:
                                    wait(v, sectok[tt - 2])
                                    wait(v, nrmtok[tt - 2])
                                c0 = tt * 16 + og * 2
                                v.tensor_scalar(rs_all[:, c0:c0 + 2], rs_all[:, c0:c0 + 2],
                                                EPS, 0.0, op0=ALU.max, op1=ALU.add).then_inc(sx.h, 1)
                                sx.n += 1
                                v.wait_ge(sx.h, sx.n)
                                v.reciprocal(rs_all[:, c0:c0 + 2],
                                             rs_all[:, c0:c0 + 2]).then_inc(sx.h, 1)
                                sx.n += 1
                                if pas == 1:
                                    v.tensor_scalar(rs_all[:, 256 + c0:256 + c0 + 2],
                                                    rs_all[:, 256 + c0:256 + c0 + 2],
                                                    EPS, 0.0, op0=ALU.max, op1=ALU.add).then_inc(sx.h, 1)
                                    sx.n += 1
                                    v.wait_ge(sx.h, sx.n)
                                    v.reciprocal(rs_all[:, 256 + c0:256 + c0 + 2],
                                                 rs_all[:, 256 + c0:256 + c0 + 2]).then_inc(sx.h, 1)
                                    sx.n += 1
                                v.wait_ge(sx.h, sx.n)
                                for hh in range(2):
                                    h = og * 2 + hh
                                    base = (gi % 2) * 512 + hh * 192
                                    v.tensor_scalar(nrm_ev[:, (tt % 2) * 1024 + h * 64:
                                                           (tt % 2) * 1024 + (h + 1) * 64],
                                                    ps_b0[:, base:base + 64],
                                                    rs_all[:, c0 + hh:c0 + hh + 1], 0.0,
                                                    op0=ALU.mult, op1=ALU.add).then_inc(sx.h, 1)
                                    sx.n += 1
                                    if pas == 1:
                                        v.tensor_scalar(sec_ev[:, (tt % 2) * 2048 + h * 128:
                                                               (tt % 2) * 2048 + (h + 1) * 128],
                                                        ps_b0[:, base + 64:base + 192],
                                                        rs_all[:, 256 + c0 + hh:256 + c0 + hh + 1], 0.0,
                                                        op0=ALU.mult, op1=ALU.add).then_inc(sx.h, 1)
                                        sx.n += 1

                        @blk.sync
                        def _(sy):
                            secdst = "gt1" if pas == 0 else "v1s"
                            nrmdst = "q1s" if pas == 0 else "k1s"
                            for tt in range(16):
                                gi_last = tt * 8 + 7
                                sy.wait_ge(ac.h, ac_g[gi_last])
                                sy.wait_ge(sx.h, sx_g[gi_last])
                                dmx(sy, c.ap(secdst)[sq, tt * 128:(tt + 1) * 128, :],
                                    sec_ev[:, (tt % 2) * 2048:(tt % 2) * 2048 + 2048], str_[tt % 2])
                                dmx(sy, c.ap(nrmdst)[sq, tt * 128:(tt + 1) * 128, :],
                                    nrm_ev[:, (tt % 2) * 1024:(tt % 2) * 1024 + 1024], str_[2 + tt % 2])
                            for r in str_:
                                sy.wait_ge(r.h, r.n)

        c.misc[f"b0st{sq}"] = [(r.h, r.n) for r in str_]


def phase_B1(c):
    nc = c.nc
    pe, ev, sx, ac = c.s["pe"], c.s["ev"], c.s["sx"], c.s["ac"]
    wlr, glr, str_ = c.r["w"], c.r["g"], c.r["s"]
    EPS = 1e-12
    import contextlib
    for sq in range(BPC):
        with contextlib.ExitStack() as _es:
            prep_sb = _es.enter_context(nc.sbuf_tensor(U("prep_sb"), [128, 304], F32))
            kT2 = _es.enter_context(nc.sbuf_tensor(U("kT2"), [128, 2 * S], BF16))
            qT2 = _es.enter_context(nc.sbuf_tensor(U("qT2"), [128, 2 * S], BF16))
            v_sb = _es.enter_context(nc.sbuf_tensor(U("v_sb"), [128, 2 * 16 * 256], BF16))
            att_sb = _es.enter_context(nc.sbuf_tensor(U("att_sb"), [128, 2 * 16 * 512], BF16))
            tmp_sb = _es.enter_context(nc.sbuf_tensor(U("tmp_sb"), [128, 4 * 512], BF16))
            ctxa = _es.enter_context(nc.sbuf_tensor(U("ctxa"), [128, 16 * 2048], BF16))
            css = _es.enter_context(nc.sbuf_tensor(U("css"), [128, 256], F32))
            cjunk = _es.enter_context(nc.sbuf_tensor(U("cjunk"), [128, 2 * 128], F32))
            nrm2 = _es.enter_context(nc.sbuf_tensor(U("nrm2"), [128, 256], F32))
            tnh2 = _es.enter_context(nc.sbuf_tensor(U("tnh2"), [128, 256], F32))
            sc3 = _es.enter_context(nc.sbuf_tensor(U("sc3"), [128, 256], F32))
            gsil_sb = _es.enter_context(nc.sbuf_tensor(U("gsil_sb"), [128, 2 * 2048], BF16))
            gtan_sb = _es.enter_context(nc.sbuf_tensor(U("gtan_sb"), [128, 2 * 2048], BF16))
            om_ev = _es.enter_context(nc.sbuf_tensor(U("om_ev"), [128, 2 * 2048], BF16))
            ps_sc = _es.enter_context(nc.psum_tensor(U("ps_sc"), [128, 4 * 512], F32))
            ps_cx = _es.enter_context(nc.psum_tensor(U("ps_cx"), [128, 2 * 512], F32))
            pe_b, ac_b, sx_b = pe.n, ac.n, sx.n
            SCG = []
            CXG = []
            for hp in range(8):
                for hh in range(2):
                    for qb in range(4):
                        for ktc in range(16):
                            SCG.append((hp, hh, qb, ktc))
                        for qt in range(4):
                            CXG.append((hp, hh, qb, qt))
            pcount = pe_b
            sc_done, cx_done = {}, {}
            blk_last_pe = {}
            for hp in range(8):
                for hh in range(2):
                    for qb in range(4):
                        for ktc in range(16):
                            pcount += 1
                            sc_done[(hp, hh, qb, ktc)] = pcount
                        for qt in range(4):
                            pcount += 1
                            cx_done[(hp, hh, qb, qt)] = pcount
                blk_last_pe[hp] = pcount
            acount = ac_b
            relu_done, csq_done = {}, {}
            for hp in range(8):
                for hh in range(2):
                    for qb in range(4):
                        for ktc in range(16):
                            acount += 1
                            relu_done[(hp, hh, qb, ktc)] = acount
                        for qt in range(4):
                            acount += 1
                            csq_done[(hp, hh, qb, qt)] = acount
            scount = sx_b
            am_done, cc_done = {}, {}
            for hp in range(8):
                for hh in range(2):
                    for qb in range(4):
                        for ktc in range(16):
                            scount += 1
                            am_done[(hp, hh, qb, ktc)] = scount
                        for qt in range(4):
                            scount += 1
                            cc_done[(hp, hh, qb, qt)] = scount
            pref = (sq == 0 and c.misc.get("b0st0") is not None)
            ldtok = {}
            wc = [wlr[0].n, wlr[1].n]
            for hp in range(8):
                if pref and hp < 2:
                    ldtok[hp] = (glr[2 + hp].h, glr[2 + hp].n + 16 * 18)
                    continue
                wc[hp % 2] += 16 * (2 + 16)
                ldtok[hp] = (wlr[hp % 2].h, wc[hp % 2])

            with nc.Block() as blk:
                if pref:
                    @blk.gpsimd
                    def _(g):
                        for t_ in c.misc["b0st0"]:
                            wait(g, t_)
                        for hp in range(2):
                            r = glr[2 + hp]
                            for tc in range(16):
                                dmx(g, v_sb[:, hp * 4096 + tc * 256: hp * 4096 + (tc + 1) * 256],
                                    c.ap("v1s")[sq, tc * 128:(tc + 1) * 128,
                                                hp * 256:(hp + 1) * 256], r)

                    @blk.scalar
                    def _(sc):
                        for t_ in c.misc["b0st0"]:
                            wait(sc, t_)
                        for hp in range(2):
                            r = glr[2 + hp]
                            dmx(sc, kT2[:, hp * S:(hp + 1) * S],
                                c.ap("k1s")[sq, 0:S, hp * 128:(hp + 1) * 128], r, transpose=True)
                            dmx(sc, qT2[:, hp * S:(hp + 1) * S],
                                c.ap("q1s")[sq, 0:S, hp * 128:(hp + 1) * 128], r, transpose=True)

                @blk.sync
                def _(sy):
                    tok = dmx(sy, prep_sb[:, :], c.ap("prep")[:, :], wlr[3])
                    wait(sy, tok)
                    for hp in range(8):
                        if pref and hp < 2:
                            continue
                        if hp >= 2:
                            sy.wait_ge(pe.h, blk_last_pe[hp - 2])
                        buf = hp % 2
                        dmx(sy, kT2[:, buf * S:(buf + 1) * S],
                            c.ap("k1s")[sq, 0:S, hp * 128:(hp + 1) * 128], wlr[buf], transpose=True)
                        dmx(sy, qT2[:, buf * S:(buf + 1) * S],
                            c.ap("q1s")[sq, 0:S, hp * 128:(hp + 1) * 128], wlr[buf], transpose=True)
                        for tc in range(16):
                            dmx(sy, v_sb[:, buf * 4096 + tc * 256: buf * 4096 + (tc + 1) * 256],
                                c.ap("v1s")[sq, tc * 128:(tc + 1) * 128,
                                            hp * 256:(hp + 1) * 256], wlr[buf])

                def _pe_score(t, hp, hh, qb, ktc):
                    buf = hp % 2
                    gi = ((hp * 2 + hh) * 4 + qb) * 16 + ktc
                    if gi >= 4:
                        t.wait_ge(ac.h, relu_done[SCG[gi - 4]])
                    t.matmul(ps_sc[:, (gi % 4) * 512:(gi % 4) * 512 + 512],
                             kT2[hh * 64:(hh + 1) * 64,
                                 buf * S + ktc * 128: buf * S + (ktc + 1) * 128],
                             qT2[hh * 64:(hh + 1) * 64,
                                 buf * S + qb * 512: buf * S + (qb + 1) * 512],
                             start=True, stop=True).then_inc(pe.h, 1)
                    pe.n += 1

                def _pe_ctx(t, hp, hh, qb, qt):
                    buf = hp % 2
                    abuf = (hh * 4 + qb) % 2
                    ci = ((hp * 2 + hh) * 4 + qb) * 4 + qt
                    if ci >= 2:
                        pk = CXG[ci - 2]
                        t.wait_ge(ac.h, csq_done[pk])
                        t.wait_ge(sx.h, cc_done[pk])
                    if qt == 0:
                        t.wait_ge(sx.h, am_done[(hp, hh, qb, 15)])
                    for ktc in range(16):
                        mm = t.matmul(ps_cx[:, (ci % 2) * 512:(ci % 2) * 512 + 128],
                                      att_sb[:, abuf * 8192 + ktc * 512 + qt * 128:
                                             abuf * 8192 + ktc * 512 + (qt + 1) * 128],
                                      v_sb[:, buf * 4096 + ktc * 256 + hh * 128:
                                           buf * 4096 + ktc * 256 + (hh + 1) * 128],
                                      start=(ktc == 0), stop=(ktc == 15))
                    mm.then_inc(pe.h, 1)
                    pe.n += 1

                @blk.tensor
                def _(t):
                    for hp in range(8):
                        wait(t, ldtok[hp])
                        for hh in range(2):
                            for qb in range(4):
                                for ktc in range(16):
                                    _pe_score(t, hp, hh, qb, ktc)
                                for qt in range(4):
                                    _pe_ctx(t, hp, hh, qb, qt)

                @blk.scalar
                def _(sc):
                    for hp in range(8):
                        for hh in range(2):
                            h = hp * 2 + hh
                            for qb in range(4):
                                for ktc in range(16):
                                    gi = ((hp * 2 + hh) * 4 + qb) * 16 + ktc
                                    sc.wait_ge(pe.h, sc_done[(hp, hh, qb, ktc)])
                                    if gi >= 4:
                                        pk = SCG[gi - 4]
                                        sc.wait_ge(sx.h, am_done[pk])
                                        sc.wait_ge(ac.h, relu_done[pk])
                                    sc.activation(tmp_sb[:, (gi % 4) * 512:(gi % 4) * 512 + 512],
                                                  ps_sc[:, (gi % 4) * 512:(gi % 4) * 512 + 512],
                                                  AF.Relu, bias=prep_sb[:, 16 + h:16 + h + 1],
                                                  scale=prep_sb[:, h:h + 1]).then_inc(ac.h, 1)
                                    ac.n += 1
                                for qt in range(4):
                                    ci = ((hp * 2 + hh) * 4 + qb) * 4 + qt
                                    sc.wait_ge(pe.h, cx_done[(hp, hh, qb, qt)])
                                    if ci >= 2:
                                        pk = CXG[ci - 2]
                                        sc.wait_ge(ac.h, csq_done[pk])
                                    sc.activation(cjunk[:, (ci % 2) * 128:(ci % 2) * 128 + 128],
                                                  ps_cx[:, (ci % 2) * 512:(ci % 2) * 512 + 128],
                                                  AF.Square,
                                                  accum_out=css[:, (qb * 4 + qt) * 16 + h:
                                                                (qb * 4 + qt) * 16 + h + 1]).then_inc(ac.h, 1)
                                    ac.n += 1

                @blk.vector
                def _(v):
                    for hp in range(8):
                        for hh in range(2):
                            for qb in range(4):
                                abuf = (hh * 4 + qb) % 2
                                for ktc in range(16):
                                    gi = ((hp * 2 + hh) * 4 + qb) * 16 + ktc
                                    v.wait_ge(ac.h, relu_done[(hp, hh, qb, ktc)])
                                    bi = (hp * 2 + hh) * 4 + qb
                                    if bi >= 2 and ktc == 0:
                                        pb = bi - 2
                                        v.wait_ge(pe.h, cx_done[(pb // 8, (pb % 8) // 4, pb % 4, 3)])
                                    v.tensor_tensor(att_sb[:, abuf * 8192 + ktc * 512:
                                                           abuf * 8192 + (ktc + 1) * 512],
                                                    tmp_sb[:, (gi % 4) * 512:(gi % 4) * 512 + 512],
                                                    tmp_sb[:, (gi % 4) * 512:(gi % 4) * 512 + 512],
                                                    op=ALU.mult).then_inc(sx.h, 1)
                                    sx.n += 1
                                for qt in range(4):
                                    ci = ((hp * 2 + hh) * 4 + qb) * 4 + qt
                                    v.wait_ge(pe.h, cx_done[(hp, hh, qb, qt)])
                                    v.wait_ge(ac.h, csq_done[(hp, hh, qb, qt)])
                                    h = hp * 2 + hh
                                    v.tensor_copy(ctxa[:, (qb * 4 + qt) * 2048 + h * 128:
                                                       (qb * 4 + qt) * 2048 + (h + 1) * 128],
                                                  ps_cx[:, (ci % 2) * 512:(ci % 2) * 512 + 128]).then_inc(sx.h, 1)
                                    sx.n += 1

            with nc.Block() as blk:
                @blk.scalar
                def _(sc):
                    sc.activation(nrm2[:, :], css[:, :], AF.Sqrt).then_inc(ac.h, 1)
                    ac.n += 1

                @blk.vector
                def _(v):
                    v.wait_ge(ac.h, ac.n)
                    v.tensor_scalar(sc3[:, :], nrm2[:, :], EPS, 0.0,
                                    op0=ALU.max, op1=ALU.add).then_inc(sx.h, 1)
                    sx.n += 1
                    v.wait_ge(sx.h, sx.n)
                    v.reciprocal(sc3[:, :], sc3[:, :]).then_inc(sx.h, 1)
                    sx.n += 1

                @blk.scalar
                def _(sc):
                    sc.activation(tnh2[:, :], nrm2[:, :], AF.Tanh).then_inc(ac.h, 1)
                    ac.n += 1

                @blk.vector
                def _(v):
                    v.wait_ge(ac.h, ac.n)
                    v.wait_ge(sx.h, sx.n)
                    v.tensor_tensor(sc3[:, :], sc3[:, :], tnh2[:, :], op=ALU.mult).then_inc(sx.h, 1)
                    sx.n += 1
                    v.wait_ge(sx.h, sx.n)
                    v.tensor_tensor(sc3[:, :], sc3[:, :], prep_sb[:, 48:304], op=ALU.mult).then_inc(sx.h, 1)
                    sx.n += 1

            with nc.Block() as blk:
                gtok = {}
                gc = [glr[0].n, glr[1].n]
                for qtg in range(16):
                    gc[qtg % 2] += 16
                    gtok[qtg] = (glr[qtg % 2].h, gc[qtg % 2])
                omtok = {}
                stc = [r.n for r in str_]
                for qtg in range(16):
                    stc[qtg % 4] += 16
                    omtok[qtg] = (str_[qtg % 4].h, stc[qtg % 4])
                ac_b2 = ac.n
                sx_b2 = sx.n

                @blk.gpsimd
                def _(g):
                    for qtg in range(16):
                        if qtg >= 2:
                            g.wait_ge(ac.h, ac_b2 + qtg - 1)
                        dmx(g, gsil_sb[:, (qtg % 2) * 2048:(qtg % 2) * 2048 + 2048],
                            c.ap("gt1")[sq, qtg * 128:(qtg + 1) * 128, :], glr[qtg % 2])

                @blk.scalar
                def _(sc):
                    for qtg in range(16):
                        wait(sc, gtok[qtg])
                        if qtg >= 2:
                            sc.wait_ge(sx.h, sx_b2 + (qtg - 1) * 16)
                        sc.activation(gtan_sb[:, (qtg % 2) * 2048:(qtg % 2) * 2048 + 2048],
                                      gsil_sb[:, (qtg % 2) * 2048:(qtg % 2) * 2048 + 2048],
                                      AF.Tanh).then_inc(ac.h, 1)
                        ac.n += 1

                @blk.vector
                def _(v):
                    for qtg in range(16):
                        v.wait_ge(ac.h, ac_b2 + qtg + 1)
                        if qtg >= 2:
                            wait(v, omtok[qtg - 2])
                        for h in range(16):
                            v.scalar_tensor_tensor(om_ev[:, (qtg % 2) * 2048 + h * 128:
                                                         (qtg % 2) * 2048 + (h + 1) * 128],
                                                   ctxa[:, qtg * 2048 + h * 128:
                                                        qtg * 2048 + (h + 1) * 128],
                                                   sc3[:, qtg * 16 + h:qtg * 16 + h + 1],
                                                   gtan_sb[:, (qtg % 2) * 2048 + h * 128:
                                                           (qtg % 2) * 2048 + (h + 1) * 128],
                                                   op0=ALU.mult, op1=ALU.mult).then_inc(sx.h, 1)
                            sx.n += 1

                @blk.sync
                def _(sy):
                    for qtg in range(16):
                        sy.wait_ge(sx.h, sx_b2 + (qtg + 1) * 16)
                        dmx(sy, c.ap("om1")[sq, qtg * 128:(qtg + 1) * 128, :],
                            om_ev[:, (qtg % 2) * 2048:(qtg % 2) * 2048 + 2048], str_[qtg % 4])
                    for r in str_:
                        sy.wait_ge(r.h, r.n)

        with (
            nc.sbuf_tensor(U("omT"), [128, NKC * S], BF16) as omT,
            nc.sbuf_tensor(U("wo_sb"), [128, NKC * D], BF16) as wo_sb,
            nc.sbuf_tensor(U("x2_sb"), [128, 2 * D], F32) as x2_sb,
            nc.sbuf_tensor(U("x3_ev"), [128, 4 * 512], F32) as x3_ev,
            nc.psum_tensor(U("ps_wo"), [128, 2 * 512], F32) as ps_wo,
        ):
            with nc.Block() as blk:
                @blk.sync
                def _(sy):
                    for kc in range(NKC):
                        tok = dmx(sy, omT[:, kc * S:(kc + 1) * S],
                                  c.ap("om1")[sq, 0:S, kc * 128:(kc + 1) * 128],
                                  wlr[0], transpose=True)
                    wait(sy, tok)

                @blk.gpsimd
                def _(g):
                    for kc in range(NKC):
                        tok = dmx(g, wo_sb[:, kc * D:(kc + 1) * D],
                                  c.ap("wo")[kc * 128:(kc + 1) * 128, :], glr[0])
                    wait(g, tok)

            with nc.Block() as blk:
                pe_b2, ev_b2 = pe.n, ev.n
                x2tok = {}
                gc2 = [glr[2].n, glr[3].n]
                for tt in range(16):
                    gc2[tt % 2] += 16
                    x2tok[tt] = (glr[2 + tt % 2].h, gc2[tt % 2])
                sttok = {}
                stc2 = [r.n for r in str_]
                for gg in range(64):
                    stc2[gg % 4] += 16
                    sttok[gg] = (str_[gg % 4].h, stc2[gg % 4])

                @blk.gpsimd
                def _(g):
                    for tt in range(16):
                        gi0 = tt * 4
                        if tt >= 2:
                            g.wait_ge(ev.h, ev_b2 + (tt - 1) * 4)
                        dmx(g, x2_sb[:, (tt % 2) * D:(tt % 2) * D + D],
                            c.ap("x2")[sq * S + tt * 128: sq * S + (tt + 1) * 128, :],
                            glr[2 + tt % 2])

                @blk.tensor
                def _(t):
                    for tt in range(16):
                        for ob in range(4):
                            gi = tt * 4 + ob
                            if gi >= 2:
                                t.wait_ge(ev.h, ev_b2 + gi - 1)
                            for kc in range(NKC):
                                mm = t.matmul(ps_wo[:, (gi % 2) * 512:(gi % 2) * 512 + 512],
                                              omT[:, kc * S + tt * 128: kc * S + (tt + 1) * 128],
                                              wo_sb[:, kc * D + ob * 512: kc * D + (ob + 1) * 512],
                                              start=(kc == 0), stop=(kc == NKC - 1))
                            mm.then_inc(pe.h, 1)
                            pe.n += 1

                @blk.vector
                def _(v):
                    for tt in range(16):
                        wait(v, x2tok[tt])
                        for ob in range(4):
                            gi = tt * 4 + ob
                            v.wait_ge(pe.h, pe_b2 + gi + 1)
                            if gi >= 4:
                                wait(v, sttok[gi - 4])
                            v.tensor_tensor(x3_ev[:, (gi % 4) * 512:(gi % 4) * 512 + 512],
                                            ps_wo[:, (gi % 2) * 512:(gi % 2) * 512 + 512],
                                            x2_sb[:, (tt % 2) * D + ob * 512:
                                                  (tt % 2) * D + (ob + 1) * 512],
                                            op=ALU.add).then_inc(ev.h, 1)
                            ev.n += 1

                @blk.sync
                def _(sy):
                    for tt in range(16):
                        for ob in range(4):
                            gi = tt * 4 + ob
                            sy.wait_ge(ev.h, ev_b2 + gi + 1)
                            dmx(sy, c.ap("x3")[sq * S + tt * 128: sq * S + (tt + 1) * 128,
                                               ob * 512:(ob + 1) * 512],
                                x3_ev[:, (gi % 4) * 512:(gi % 4) * 512 + 512], str_[gi % 4])
                    for r in str_:
                        sy.wait_ge(r.h, r.n)


from concourse.bass_utils import run_bass_kernel_spmd
import contextlib

NCORES = 8

LAUNCH_GROUPS = [list(PHASE_ORDER)]

PHASE_FN = {
    "W0": phase_W0, "W1": phase_W1, "W3": phase_W3,
    "A0": phase_A0, "A1": phase_A1,
    "F0a": phase_F0a, "F0b": phase_F0b,
    "B0": phase_B0, "B1": phase_B1,
    "F1a": phase_F1a, "F1b": phase_F1b,
}


def build_launch(phases, lut):
    sel = [p for p in PHASE_ORDER if p in phases]
    all_sel = set()
    for g in LAUNCH_GROUPS:
        all_sel.update(g)
    nc = bass.Bass("TRN2", target_bir_lowering=False)
    handles = {}
    in_names, out_names = [], []
    needed_inputs = set()
    for p in sel:
        needed_inputs.update(PHASE_INPUTS[p])
    for name, (shape, dt, prod, cons) in SCRATCH.items():
        used = any(p in sel for p in cons) or prod in sel
        if not used:
            continue
        if prod in sel:
            consumed_outside = any(p not in sel for p in cons)
            kind = "ExternalOutput" if consumed_outside else "Internal"
            handles[name] = nc.dram_tensor(name, shape, dt, kind=kind)
            if kind == "ExternalOutput":
                out_names.append(name)
        elif any(p in sel for p in cons):
            handles[name] = nc.dram_tensor(name, shape, dt, kind="ExternalInput")
            in_names.append(name)
    for name in sorted(needed_inputs):
        shape, dt = MODULE_INPUTS[name]
        handles[name] = nc.dram_tensor(name, shape, dt, kind="ExternalInput")
        in_names.append(name)
    if "F1b" in sel:
        handles["y"] = nc.dram_tensor("y", [BPC, S, D], F32, kind="ExternalOutput")
        out_names.append("y")
    with contextlib.ExitStack() as stack:
        def mksem(nm):
            return Sem(stack.enter_context(nc.semaphore(nm)))
        sems = {sn: mksem(sn) for sn in ["pe", "ev", "sx", "ac", "pl", "hx", "wc0", "wc1", "wc2"]}
        rings = {
            "g": [mksem(f"g{i}") for i in range(4)],
            "w": [mksem(f"w{i}") for i in range(4)],
            "s": [mksem(f"s{i}") for i in range(4)],
            "a": [mksem(f"a{i}") for i in range(4)],
            "p": [mksem(f"p{i}") for i in range(4)],
            "hl": [mksem(f"hl{i}") for i in range(2)],
            "hs": [mksem(f"hs{i}") for i in range(2)],
            }
        c = Ctx(nc, handles, sems, rings, lut)
        if {"A0", "F0a", "F1a", "F1b", "W3"}.issubset(set(sel)):
            c.ghost = Ghost(c)
            c.init_cast_jobs()
        ident = nc.alloc_sbuf_tensor("ident_bf", [128, 128], BF16)
        c.ident = ident
        with nc.Block() as blk:
            @blk.gpsimd
            def _(g):
                pl = sems["pl"]
                g.memset(ident[:, :], 0.0).then_inc(pl.h, 1)
                pl.n += 1
                g.wait_ge(pl.h, pl.n)
                g.affine_select(out=ident[:, :], in_=ident[:, :],
                                compare_op=mybir.AluOpType.not_equal, fill=1.0, base=0,
                                pattern=[[-1, 128]], channel_multiplier=1).then_inc(pl.h, 1)
                pl.n += 1
        for p in sel:
            PHASE_FN[p](c)
    return nc, in_names, out_names


def kernel(**inputs):
    lut = np.asarray(inputs["L1_ffn_lut"], dtype=np.float32)
    per_core_mod = []
    shared = {
        "w_qkv": np.ascontiguousarray(inputs["L0_attn_w_qkv"]),
        "b_qkv": np.ascontiguousarray(inputs["L0_attn_b_qkv"]),
        "w_out": np.ascontiguousarray(inputs["L0_attn_w_out"]),
        "b_out": np.ascontiguousarray(inputs["L0_attn_b_out"]),
        "w_gup": np.ascontiguousarray(inputs["L0_ffn_w_gate_up"]),
        "w_down": np.ascontiguousarray(inputs["L0_ffn_w_down"]),
        "wqg": np.ascontiguousarray(inputs["L1_attn_Wqg"]),
        "wkv": np.ascontiguousarray(inputs["L1_attn_Wkv"]),
        "wo": np.ascontiguousarray(inputs["L1_attn_Wo"]),
        "ls_iaw": np.ascontiguousarray(inputs["L1_attn_ls_iaw"]),
        "ls_hws": np.ascontiguousarray(inputs["L1_attn_ls_hws"]),
        "gidx": np.ascontiguousarray(inputs["L1_ffn_gate_idx"]),
        "uidx": np.ascontiguousarray(inputs["L1_ffn_up_idx"]),
        "didx": np.ascontiguousarray(inputs["L1_ffn_down_idx"]),
    }
    x_full = np.ascontiguousarray(inputs["L0_attn_hidden_states"])
    for core in range(NCORES):
        m = dict(shared)
        m["x"] = np.ascontiguousarray(x_full[2 * core:2 * core + 2])
        per_core_mod.append(m)

    scratch_store = [dict() for _ in range(NCORES)]
    y_out = None
    for grp in LAUNCH_GROUPS:
        nc, in_names, out_names = build_launch(grp, lut)
        in_maps = []
        for core in range(NCORES):
            m = {}
            for n in in_names:
                if n in per_core_mod[core]:
                    m[n] = per_core_mod[core][n]
                else:
                    m[n] = scratch_store[core][n]
            in_maps.append(m)
        res = run_bass_kernel_spmd(nc, in_maps, core_ids=list(range(NCORES)))
        for core in range(NCORES):
            for n in out_names:
                scratch_store[core][n] = np.asarray(res.results[core][n])
        if "y" in out_names:
            y_out = [np.asarray(res.results[core]["y"], dtype=np.float32)
                     for core in range(NCORES)]
    return np.concatenate(y_out, axis=0)
